# version 12
# speedup vs baseline: 1.1995x; 1.0048x over previous
import contextlib
import math
import numpy as np
import ml_dtypes
import concourse.bass as bass
import concourse.mybir as mybir
from concourse.bass_utils import run_bass_kernel_spmd
from concourse.alu_op_type import AluOpType as OP

F32 = mybir.dt.float32
BF16 = mybir.dt.bfloat16
I32 = mybir.dt.int32
AF = mybir.ActivationFunctionType
AX = mybir.AxisListType
BF = ml_dtypes.bfloat16

NEG = -1e30
LN_EPS = 1e-5


def cdiv(a, b):
    return (a + b - 1) // b


class S:
    def __init__(self, es, nc, name):
        self.h = es.enter_context(nc.semaphore(name))
        self.n = 0
        self.base = 0

    def wait_at(self, eng, k):
        eng.wait_ge(self.h, self.base + k)

    def inc(self, instr, k=1):
        instr.then_inc(self.h, k)
        self.n += k
        return self

    def dinc(self, instr):
        return self.inc(instr, 16)

    def wait(self, eng):
        eng.wait_ge(self.h, self.n)


class Ctx:
    def __init__(self, nc, es):
        self.nc = nc
        self.es = es
        self.es_global = es
        self.g = nc.gpsimd
        self.sy = nc.sync
        self.sc = nc.scalar
        self.v = nc.vector
        self.t = nc.tensor
        self._n = 0

    def sb(self, name, shape, dt, es=None):
        self._n += 1
        return (es or self.es).enter_context(
            self.nc.sbuf_tensor(f"{name}.{self._n}", shape, dt))

    def ps(self, name, shape, dt=F32, es=None):
        self._n += 1
        return (es or self.es).enter_context(
            self.nc.psum_tensor(f"{name}.{self._n}", shape, dt))

    def stage_scope(self):
        es = contextlib.ExitStack()
        prev = self.es
        self.es = es
        es.callback(lambda: setattr(self, "es", prev))
        return es

    POOLS = {"c": 50, "g": 20, "h": 20}

    def sem(self, name=None, kind="c"):
        if not hasattr(self, "_pool"):
            self._pool = {k: [] for k in self.POOLS}
            self._pk = {k: 0 for k in self.POOLS}
        pool, n = self._pool[kind], self.POOLS[kind]
        if len(pool) < n:
            s = S(self.es_global, self.nc, f"sp_{kind}{len(pool)}")
            pool.append(s)
            self._pk[kind] = len(pool)
        else:
            s = pool[self._pk[kind] % n]
            self._pk[kind] += 1
        s.base = s.n
        return s

    def init_consts(self, values):
        vals = sorted(set(float(x) for x in values))
        if not vals:
            return
        tile = self.sb("cst", [128, len(vals)], F32)
        for i, val in enumerate(vals):
            self.v.memset(tile[:, i:i + 1], val)
            self.nc.const_aps.register(F32, val, tile[:, i:i + 1])
        self.barrier([self.v], engs(self, ALL), "cst_b")

    def barrier(self, producers, consumers, name=None):
        s = self.sem(name or "bar")
        for e in producers:
            e.drain()
            e.sem_inc(s.h, 1)
            s.n += 1
        for e in consumers:
            s.wait(e)
        return s


ALL = "all"


def engs(c, which):
    if which == ALL:
        return [c.g, c.sy, c.sc, c.v, c.t]
    return which


def _stage(c, io, consts):
    if c is None:
        nc = bass.Bass("TRN2", target_bir_lowering=False)
        es = contextlib.ExitStack()
        c = Ctx(nc, es)
        c.init_consts(consts)
        return c, es, ({} if io is None else io), True
    return c, c.stage_scope(), (io or {}), False


PAIRS = [[0, 1], [2, 3], [4, 5], [6, 7]]


def emit_cc(c, kind, ins, outs, groups, op=None):
    c._ncc = getattr(c, "_ncc", 0) + 1
    s = S(c.es_global, c.nc, f"ccs{c._ncc}")
    aluop = op if op is not None else mybir.AluOpType.bypass
    s.inc(c.g.collective_compute(
        kind, aluop, replica_groups=groups,
        ins=[a.opt() for a in ins], outs=[a.opt() for a in outs]))
    return s


def emit_cc_wait(c, s):
    for e in engs(c, ALL):
        s.wait(e)


def emit_sel_combine(c, triples, W, fa, fb, name="sel", dt=BF16, P=128):
    v, sy = c.v, c.sy
    with contextlib.ExitStack() as les:
        at = c.sb(f"{name}_a", [128, 2 * W], dt, les)
        bt = c.sb(f"{name}_b", [128, 2 * W], dt, les)
        ot = c.sb(f"{name}_o", [128, 2 * W], dt, les)
        s_ld = [c.sem(f"{name}_l0", "h"), c.sem(f"{name}_l1", "h")]
        s_cm = c.sem(f"{name}_c")
        s_st = [c.sem(f"{name}_s0", "h"), c.sem(f"{name}_s1", "h")]
        for i, (a, b, o) in enumerate(triples):
            sl = i % 2
            W_ = slice(sl * W, sl * W + W)
            if i >= 2:
                s_cm.wait_at(sy, i - 1)
            s_ld[sl].dinc(sy.dma_start(at[0:P, W_], a))
            s_ld[sl].dinc(sy.dma_start(bt[0:P, W_], b))
            s_ld[sl].wait(v)
            if i >= 2:
                s_st[sl].wait(v)
            v.tensor_scalar(at[0:P, W_], at[0:P, W_], fa[0:P, :], None, OP.mult)
            v.tensor_scalar(bt[0:P, W_], bt[0:P, W_], fb[0:P, :], None, OP.mult)
            v.drain()
            s_cm.inc(v.tensor_tensor(ot[0:P, W_], at[0:P, W_], bt[0:P, W_], OP.add))
            s_cm.wait(sy)
            s_st[sl].dinc(sy.dma_start(o, ot[0:P, W_]))
        sy.wait_ge(s_st[0].h, s_st[0].n)
        sy.wait_ge(s_st[1].h, s_st[1].n)
        c.barrier([sy], engs(c, ALL), f"{name}_fin")


def emit_copy(c, pairs, name="cp"):
    s = c.sem(name, "h")
    for dst, src in pairs:
        s.dinc(c.sy.dma_start(dst, src))
    s.wait(c.sy)
    c.barrier([c.sy], engs(c, ALL), f"{name}_b")



def emit_ln_stats(c, es, x_hbm, D, T, name="ln"):
    nc, g, sy, sc, v, t = c.nc, c.g, c.sy, c.sc, c.v, c.t
    nd = D // 128
    ng = cdiv(T, 512)
    m_b = c.sb(f"{name}_m", [128, T], F32, es)
    r_b = c.sb(f"{name}_r", [128, T], F32, es)
    with contextlib.ExitStack() as les:
        ones = c.sb(f"{name}_ones", [128, 128], BF16, les)
        xrot = c.sb(f"{name}_xrot", [128, 2 * T], BF16, les)
        xsq = c.sb(f"{name}_xsq", [128, 2 * T], BF16, les)
        psum_s = c.ps(f"{name}_ps_s", [128, T], F32, les)
        psum_q = c.ps(f"{name}_ps_q", [128, T], F32, les)
        var = c.sb(f"{name}_var", [128, T], F32, les)
        m2 = c.sb(f"{name}_m2", [128, T], F32, les)

        s_ld = [c.sem(f"{name}_ld0", "g"), c.sem(f"{name}_ld1", "g")]
        s_sq = c.sem(f"{name}_sq")
        s_mm = c.sem(f"{name}_mm")

        v.memset(ones[:, :], 1.0)
        v.drain()
        v.sem_inc(s_mm.h, 1)
        s_mm.n += 1
        s_mm.wait_at(t, 1)

        for i in range(nd):
            sl = i % 2
            xi = xrot[:, sl * T:sl * T + T]
            if i >= 2:
                s_mm.wait_at(g, i)
            s_ld[sl].dinc(g.dma_start(xi, x_hbm[i * 128:(i + 1) * 128, :]))
            s_ld[sl].wait(sc)
            if i >= 2:
                s_mm.wait_at(sc, i)
            sq = xsq[:, sl * T:sl * T + T]
            s_sq.inc(sc.activation(sq, xi, AF.Square))
            t.wait_ge(s_sq.h, s_sq.n)
            s_ld[sl].wait(t)
            mm2 = None
            for gi in range(ng):
                t0, t1 = gi * 512, min(T, (gi + 1) * 512)
                t.matmul(psum_s[:, t0:t1], ones[:, :], xi[:, t0:t1],
                         start=(i == 0), stop=(i == nd - 1))
                mm2 = t.matmul(psum_q[:, t0:t1], ones[:, :], sq[:, t0:t1],
                               start=(i == 0), stop=(i == nd - 1))
            s_mm.inc(mm2)
        s_mm.wait(sc)
        sc.mul(m_b[:, :], psum_s[:, :], 1.0 / D)
        sc.drain()
        sc.mul(var[:, :], psum_q[:, :], 1.0 / D)
        sc.drain()
        sc.activation(m2[:, :], m_b[:, :], AF.Square)
        c.barrier([sc], [c.v], f"{name}_b1")
        v.tensor_tensor(var[:, :], var[:, :], m2[:, :], OP.subtract)
        c.barrier([v], [sc], f"{name}_b2")
        sc.activation(var[:, :], var[:, :], AF.Ln, bias=LN_EPS)
        sc.drain()
        sc.activation(r_b[:, :], var[:, :], AF.Exp, scale=-0.5)
        c.barrier([sc], engs(c, ALL), f"{name}_b3")
    return m_b, r_b


def emit_u(c, x_hbm, u, D, tok0, TW, m_b, r_b, name="u", ustride=None, uoff=0):
    nc, g, sy, sc, v, t = c.nc, c.g, c.sy, c.sc, c.v, c.t
    nd = D // 128
    if ustride is None:
        ustride = TW
    with contextlib.ExitStack() as les:
        xrot = c.sb(f"{name}_xrot", [128, 2 * TW], BF16, les)
        s_ld = [c.sem(f"{name}_ld0", "g"), c.sem(f"{name}_ld1", "g")]
        s_u = c.sem(f"{name}_u")
        for i in range(nd):
            sl = i % 2
            xi = xrot[:, sl * TW:sl * TW + TW]
            if i >= 2:
                s_u.wait_at(g, 2 * (i - 1))
            s_ld[sl].dinc(g.dma_start(xi, x_hbm[i * 128:(i + 1) * 128,
                                                tok0:tok0 + TW]))
            s_ld[sl].wait(v)
            ui = u[:, i * ustride + uoff:i * ustride + uoff + TW]
            v.tensor_tensor(ui, xi, m_b[:, tok0:tok0 + TW], OP.subtract)
            v.drain()
            s_u.inc(v.tensor_tensor(ui, ui, r_b[:, tok0:tok0 + TW], OP.mult), 2)
        c.barrier([v], engs(c, ALL), f"{name}_b")



def emit_proj(c, u, T_in, w_hbm, nd, nout, out_sb, TW, tok0=0,
              bias_sb=None, bias_off=0, act=None, name="p",
              out_stride=None, cast=True, accum_sb=None, out_off=0):
    nc, g, sy, sc, v, t = c.nc, c.g, c.sy, c.sc, c.v, c.t
    ng = cdiv(TW, 512)
    out_stride = out_stride or TW
    ld_q = g if cast else sy
    with contextlib.ExitStack() as les:
        wrot = c.sb(f"{name}_w", [128, 2 * nd * 128], BF16, les)
        psum = c.ps(f"{name}_ps", [128, 2 * 512], F32, les)
        kq = "g" if cast else "h"
        s_w = [c.sem(f"{name}_sw0", kq), c.sem(f"{name}_sw1", kq)]
        s_mm = c.sem(f"{name}_smm")
        s_ev = c.sem(f"{name}_sev")
        nmm = 0
        for j in range(nout):
            wj = wrot[:, (j % 2) * nd * 128:(j % 2) * nd * 128 + nd * 128]
            if j >= 2:
                s_mm.wait_at(ld_q, (j - 1) * ng)
            s_w[j % 2].dinc(ld_q.dma_start(wj, w_hbm[j * 128:(j + 1) * 128, :]))
            s_w[j % 2].wait(t)
            for gi in range(ng):
                t0 = tok0 + gi * 512
                t1 = min(tok0 + TW, t0 + 512)
                pb = psum[:, (nmm % 2) * 512:(nmm % 2) * 512 + (t1 - t0)]
                if nmm >= 2:
                    s_ev.wait_at(t, nmm - 1)
                for i in range(nd):
                    mm = t.matmul(pb, wj[:, i * 128:(i + 1) * 128],
                                  u[:, i * T_in + t0:i * T_in + t1],
                                  start=(i == 0), stop=(i == nd - 1))
                s_mm.inc(mm)
                nmm += 1
                dst = out_sb[:, j * out_stride + out_off + gi * 512:
                             j * out_stride + out_off + gi * 512 + (t1 - t0)]
                if accum_sb is not None:
                    s_mm.wait(v)
                    acc = accum_sb[:, j * out_stride + out_off + gi * 512:
                                   j * out_stride + out_off + gi * 512 + (t1 - t0)]
                    if bias_sb is not None:
                        ev = v.scalar_tensor_tensor(
                            dst, pb, bias_sb[:, bias_off + j:bias_off + j + 1],
                            acc, OP.add, OP.add)
                    else:
                        ev = v.tensor_tensor(dst, pb, acc, OP.add)
                    v.drain()
                    s_ev.inc(ev)
                    continue_evac = True
                else:
                    s_mm.wait(sc)
                    if bias_sb is not None:
                        ev = sc.activation(dst, pb, act or AF.Identity,
                                           bias=bias_sb[:, bias_off + j:bias_off + j + 1])
                    elif act is not None:
                        ev = sc.activation(dst, pb, act)
                    else:
                        ev = sc.copy(dst, pb)
                    s_ev.inc(ev)
        c.barrier([sc, v] if accum_sb is not None else [sc],
                  engs(c, ALL), f"{name}_bar")



def emit_identity(c, es, name="ident"):
    ident = c.sb(name, [128, 128], BF16, es)
    g, v = c.g, c.v
    s = c.sem(name)
    g.iota(ident[:, :], [[1, 128]], channel_multiplier=-1,
           allow_small_or_imprecise_dtypes=True).then_inc(s.h, 1)
    s.n += 1
    s.wait(v)
    v.tensor_scalar(ident[:, :], ident[:, :], 0.0, None, OP.is_equal)
    c.barrier([v], [c.t], name + "_b")
    return ident


def emit_bcast_row(c, row_ap, dst_sb, T, name="bc"):
    nc, v, t = c.nc, c.v, c.t
    with contextlib.ExitStack() as les:
        onescol = c.sb(f"{name}_one", [1, 128], F32, les)
        ps = c.ps(f"{name}_ps", [128, 512], F32, les)
        v.memset(onescol[:, :], 1.0)
        c.barrier([v], [t], f"{name}_b0")
        s = c.sem(name)
        for gi in range(cdiv(T, 512)):
            t0, t1 = gi * 512, min(T, (gi + 1) * 512)
            if gi >= 1:
                s.wait_at(t, 2 * gi)
            s.inc(t.matmul(ps[:, 0:t1 - t0], onescol[:, :], row_ap[:, t0:t1],
                           start=True, stop=True))
            s.wait(v)
            s.inc(v.tensor_copy(dst_sb[:, t0:t1], ps[:, 0:t1 - t0]))
        c.barrier([v], engs(c, ALL), f"{name}_b1")


def emit_transposes(c, pairs, ident, name="tp", kdim=128):
    t, v = c.t, c.v
    with contextlib.ExitStack() as les:
        ps_t = c.ps(f"{name}_ps", [128, 2048], BF16, les)
        s_tp = c.sem(name + "p")
        s_tc = c.sem(name + "c")
        for idx, (src, dst) in enumerate(pairs):
            if idx >= 2:
                s_tc.wait_at(t, idx - 1)
            off = (idx % 2) * 1024
            s_tp.inc(t.transpose(ps_t[0:src.shape[1], off:off + src.shape[0]],
                                 src, ident[0:kdim, 0:kdim]))
            v.wait_ge(s_tp.h, s_tp.n)
            s_tc.inc(v.tensor_copy(dst, ps_t[0:src.shape[1], off:off + src.shape[0]]))
        c.barrier([v], engs(c, ALL), name + "_b")


def emit_residual_out(c, xr_hbm, add_sb, nd, TW, xout_hbm, out_dt=BF16,
                      name="res", xr_dt=BF16, add_stride=None):
    nc, g, sy, sc, v, t = c.nc, c.g, c.sy, c.sc, c.v, c.t
    add_stride = add_stride or TW
    with contextlib.ExitStack() as les:
        xrt = c.sb(f"{name}_xr", [128, 2 * TW], xr_dt, les)
        ot = c.sb(f"{name}_o", [128, 2 * TW], out_dt, les)
        s_ld = [c.sem(f"{name}_ld0", "h"), c.sem(f"{name}_ld1", "h")]
        s_add = c.sem(f"{name}_add")
        s_st = [c.sem(f"{name}_st0", "g"), c.sem(f"{name}_st1", "g")]
        for i in range(nd):
            sl = i % 2
            xi = xrt[:, sl * TW:sl * TW + TW]
            oi = ot[:, sl * TW:sl * TW + TW]
            if i >= 2:
                s_add.wait_at(sy, i - 1)
            s_ld[sl].dinc(sy.dma_start(xi, xr_hbm[i * 128:(i + 1) * 128, :]))
            s_ld[sl].wait(v)
            if i >= 2:
                s_st[sl].wait(v)
            s_add.inc(v.tensor_tensor(oi, xi, add_sb[:, i * add_stride:i * add_stride + TW],
                                      OP.add))
            g.wait_ge(s_add.h, s_add.n)
            s_st[sl].dinc(g.dma_start(xout_hbm[i * 128:(i + 1) * 128, :], oi))
        g.wait_ge(s_st[0].h, s_st[0].n)
        g.wait_ge(s_st[1].h, s_st[1].n)


def emit_attn_core(c, qsb, ksb, vtr, att, colb, rowv, ident, H, per, TH, T,
                   DH=128, att_h0=0, name="at"):
    nc, g, sy, sc, v, t = c.nc, c.g, c.sy, c.sc, c.v, c.t
    nk = T // 128
    sdh = 1.0 / math.sqrt(DH)
    with contextlib.ExitStack() as les_a:
        QGW = min(512, TH)
        nqi = QGW // 128
        msk4 = c.sb(f"{name}_msk4", [128, nqi * T], BF16, les_a)
        ps_s = c.ps(f"{name}_ps_s", [128, T], F32, les_a)
        ps_t = c.ps(f"{name}_ps_t", [128, 2048], BF16, les_a)
        ps_o = c.ps(f"{name}_ps_o", [128, QGW], F32, les_a)
        prob = c.sb(f"{name}_prob", [128, T], BF16, les_a)
        prT = c.sb(f"{name}_prT", [128, nqi * T], BF16, les_a)
        rmax = c.sb(f"{name}_rmax", [128, 1], F32, les_a)
        rsum = c.sb(f"{name}_rsum", [128, 1], F32, les_a)
        s_sc = c.sem(f"{name}_sc")
        s_red = c.sem(f"{name}_red")
        s_pb = c.sem(f"{name}_pb")
        s_tp = c.sem(f"{name}_tp")
        s_tc = c.sem(f"{name}_tc")
        s_av = c.sem(f"{name}_av")
        s_ao = c.sem(f"{name}_ao")
        for qg in range(TH // QGW):
            for qi in range(nqi):
                qt = qg * nqi + qi
                lo = qt * 128
                mq = msk4[:, qi * T + lo:(qi + 1) * T]
                v.wait_ge(s_av.h, s_av.n)
                v.tensor_scalar(mq, colb[:, lo:T], rowv[:, qt:qt + 1], None,
                                OP.is_gt)
            v.drain()
            for qi in range(nqi):
                qt = qg * nqi + qi
                mq = msk4[:, qi * T + qt * 128:(qi + 1) * T]
                v.tensor_scalar(mq, mq, NEG, None, OP.mult)
            c.barrier([v], [v, t, sc], f"{name}_bm{qg}")
            for h in range(H):
                gi_ = h // per
                kh = ksb[:, gi_ * T:(gi_ + 1) * T]
                for qi in range(nqi):
                    qt = qg * nqi + qi
                    qh = qsb[:, h * TH + qt * 128:h * TH + qt * 128 + 128]
                    t.wait_ge(s_pb.h, s_pb.n)
                    blocks = []
                    b0 = 0
                    while b0 < (qt + 1) * 128:
                        blocks.append((b0, min((qt + 1) * 128, b0 + 512)))
                        b0 += 512
                    b0 = TH
                    while b0 < T:
                        blocks.append((b0, min(T, b0 + 512)))
                        b0 += 512
                    for t0, t1 in blocks:
                        mm = t.matmul(ps_s[:, t0:t1], qh, kh[:, t0:t1],
                                      start=True, stop=True)
                    s_sc.inc(mm)
                    s_sc.wait(v)
                    lo = qt * 128
                    mid0 = (qt + 1) * 128
                    if mid0 < TH:
                        v.memset(ps_s[:, mid0:TH], NEG)
                        v.tensor_tensor(ps_s[:, lo:mid0], ps_s[:, lo:mid0],
                                        msk4[:, qi * T + lo:qi * T + mid0],
                                        OP.add)
                        v.tensor_tensor(ps_s[:, TH:T], ps_s[:, TH:T],
                                        msk4[:, qi * T + TH:(qi + 1) * T],
                                        OP.add)
                    else:
                        v.tensor_tensor(ps_s[:, lo:T], ps_s[:, lo:T],
                                        msk4[:, qi * T + lo:(qi + 1) * T],
                                        OP.add)
                    v.drain()
                    v.tensor_reduce(rmax[:, :], ps_s[:, :], AX.X, OP.max)
                    v.drain()
                    s_red.inc(v.tensor_scalar(rmax[:, :], rmax[:, :], -sdh,
                                              None, OP.mult))
                    s_red.wait(sc)
                    sc.wait_ge(s_tp.h, s_tp.n)
                    s_pb.inc(sc.activation(prob[:, :], ps_s[:, :], AF.Exp,
                                           bias=rmax[:, :], scale=sdh,
                                           accum_out=rsum[:, :]))
                    s_pb.wait(v)
                    v.reciprocal(rsum[:, :], rsum[:, :])
                    v.drain()
                    s_pb.inc(v.tensor_scalar(prob[:, :], prob[:, :],
                                             rsum[:, :], None, OP.mult))
                    s_pb.wait(t)
                    prTv = prT[:].rearrange("p (kc q) -> p kc q", kc=nk)
                    ntb = s_tc.n
                    qtg = qg * nqi + nqi - 1
                    batches = [list(range(qtg + 1)),
                               list(range(TH // 128, nk))]
                    for kb, kcl in enumerate(batches):
                        off = (kb % 2) * 1024
                        t.wait_ge(s_tc.h, max(0, ntb - 1 + kb))
                        for j, kc in enumerate(kcl):
                            mm = t.transpose(
                                ps_t[:, off + j * 128:off + (j + 1) * 128],
                                prob[:, kc * 128:(kc + 1) * 128], ident[:, :])
                        s_tp.inc(mm)
                        v.wait_ge(s_tp.h, s_tp.n)
                        k0, nn = kcl[0], len(kcl)
                        s_tc.inc(v.tensor_copy(
                            prTv[:, k0:k0 + nn, qi * 128:qi * 128 + 128],
                            ps_t[:, off:off + nn * 128].rearrange(
                                "p (a q) -> p a q", a=nn)))
                s_tc.wait(t)
                t.wait_ge(s_ao.h, s_ao.n)
                live = batches[0] + batches[1]
                for ii, kc in enumerate(live):
                    mm = t.matmul(
                        ps_o[:, 0:QGW],
                        vtr[:, (gi_ * nk + kc) * 128:(gi_ * nk + kc) * 128 + 128],
                        prT[:, kc * QGW:(kc + 1) * QGW],
                        start=(ii == 0), stop=(ii == len(live) - 1))
                s_av.inc(mm)
                s_av.wait(v)
                s_ao.inc(v.tensor_copy(
                    att[:, (att_h0 + h) * TH + qg * QGW:(att_h0 + h) * TH + qg * QGW + QGW],
                    ps_o[:, 0:QGW]))
                t.wait_ge(s_ao.h, s_ao.n)
        c.barrier([v], engs(c, ALL), f"{name}_fin")



def build_attn_a(D=2048, T=2048, TH=1024, H=16, G=4, DH=128, x_dt=F32,
                 c=None, io=None):
    per = H // G
    nd = D // 128
    nh = TH // 128
    nk = T // 128
    nq_out = H * DH // 128
    nkv_out = G * DH // 128
    sdh = 1.0 / math.sqrt(DH)
    nhalf = T // TH

    c, es, io, standalone = _stage(c, io, [LN_EPS])
    nc = c.nc
    g, sy, sc, v, t = c.g, c.sy, c.sc, c.v, c.t

    if standalone:
        io["x"] = nc.dram_tensor("x", [D, T], x_dt, kind="ExternalInput").ap()
        io["wq"] = nc.dram_tensor("wq", [H * DH, D], F32, kind="ExternalInput").ap()
        io["wk"] = nc.dram_tensor("wk", [G * DH, D], F32, kind="ExternalInput").ap()
        io["wv"] = nc.dram_tensor("wv", [G * DH, D], F32, kind="ExternalInput").ap()
        io["bq"] = nc.dram_tensor("bq", [128, nq_out], F32, kind="ExternalInput").ap()
        io["bk"] = nc.dram_tensor("bk", [128, nkv_out], F32, kind="ExternalInput").ap()
        io["bv"] = nc.dram_tensor("bv", [128, nkv_out], F32, kind="ExternalInput").ap()
        io["rowv"] = nc.dram_tensor("rowv", [128, nh], F32, kind="ExternalInput").ap()
        io["colv"] = nc.dram_tensor("colv", [1, T], F32, kind="ExternalInput").ap()
        io["xr"] = nc.dram_tensor("xr", [D, TH], x_dt, kind="ExternalInput").ap()
        io["xout"] = nc.dram_tensor("xout", [D, TH], BF16, kind="ExternalOutput").ap()
    x, wq, wk, wv = io["x"], io["wq"], io["wk"], io["wv"]
    bq_h, bk_h, bv_h = io["bq"], io["bk"], io["bv"]
    rowv_h, colv_h, xr, xout = io["rowv"], io["colv"], io["xr"], io["xout"]

    with es:
        qsb = c.sb("qsb", [128, nq_out * TH], BF16)
        ksb = c.sb("ksb", [128, nkv_out * T], BF16)
        vtr = c.sb("vtr", [128, nkv_out * T], BF16)
        att = c.sb("att", [128, nq_out * TH], BF16)
        colb = c.sb("colb", [128, T], F32)
        rowv = c.sb("rowv_s", [128, nh], F32)
        ident = emit_identity(c, es)

        ball = c.sb("ball", [128, nq_out + 2 * nkv_out], F32)
        s_b = c.sem("bias", "h")
        s_b.dinc(sy.dma_start(ball[:, 0:nq_out], bq_h[:, :]))
        s_b.dinc(sy.dma_start(ball[:, nq_out:nq_out + nkv_out], bk_h[:, :]))
        s_b.dinc(sy.dma_start(ball[:, nq_out + nkv_out:], bv_h[:, :]))
        s_b.dinc(sy.dma_start(rowv[:, :], rowv_h[:, :]))
        s_b.wait(sc)
        s_b.wait(v)

        with contextlib.ExitStack() as les_u:
            m_b, r_b = emit_ln_stats(c, les_u, x, D, T)
            u = c.sb("u", [128, nd * TH], BF16, les_u)
            with contextlib.ExitStack() as les_v:
                vsb = c.sb("vsb", [128, nkv_out * T], BF16, les_v)
                for hf in range(nhalf):
                    tok0 = hf * TH
                    emit_u(c, x, u, D, tok0, TH, m_b, r_b, name=f"u{hf}")
                    if hf == 0:
                        emit_proj(c, u, TH, wq, nd, nq_out, qsb, TH, name="q",
                                  bias_sb=ball, bias_off=0)
                    emit_proj(c, u, TH, wk, nd, nkv_out, ksb, TH, name=f"k{hf}",
                              bias_sb=ball, bias_off=nq_out,
                              out_stride=T, out_off=tok0)
                    emit_proj(c, u, TH, wv, nd, nkv_out, vsb, TH, name=f"v{hf}",
                              bias_sb=ball, bias_off=nq_out + nkv_out,
                              out_stride=T, out_off=tok0)
                emit_transposes(
                    c, [(vsb[:, oc * T + kc * 128:oc * T + kc * 128 + 128],
                         vtr[:, (oc * nk + kc) * 128:(oc * nk + kc) * 128 + 128])
                        for oc in range(nkv_out) for kc in range(nk)], ident, "vt")
            with contextlib.ExitStack() as les_c:
                colv = c.sb("colv_s", [1, T], F32, les_c)
                s_c = c.sem("cl", "h")
                s_c.dinc(sy.dma_start(colv[:, :], colv_h[:, :]))
                s_c.wait(v)
                s_c.wait(t)
                emit_bcast_row(c, colv[:, :], colb, T, "cb")

        emit_attn_core(c, qsb, ksb, vtr, att, colb, rowv, ident, H, per, TH, T)

        emit_residual_out(c, xr, att, nd, TH, xout, xr_dt=x_dt)
    return nc



def emit_rope_tables_tm(c, es, colv_row, T, JW, freq_expr, cosb, sinb,
                        name="rt"):
    nc, g, sy, sc, v, t = c.nc, c.g, c.sy, c.sc, c.v, c.t
    TWO_PI = 2.0 * math.pi
    nch = T // 128
    with contextlib.ExitStack() as les:
        jrow = c.sb(f"{name}_j", [1, JW], F32, les)
        inv2 = c.sb(f"{name}_inv", [1, JW], F32, les)
        ps = c.ps(f"{name}_ps", [128, 2 * JW], F32, les)
        red = c.sb(f"{name}_red", [128, JW], F32, les)
        qi_ = c.sb(f"{name}_qi", [128, JW], I32, les)
        sh = c.sb(f"{name}_sh", [128, JW], F32, les)
        s0 = c.sem(f"{name}_s0")
        g.iota(jrow[:, :].bitcast(I32), [[1, JW]],
               channel_multiplier=0).then_inc(s0.h, 1)
        s0.n += 1
        s0.wait(v)
        freq_expr(jrow, v, sc, c)
        c.barrier([v, sc], [sc], f"{name}_b0")
        sc.activation(inv2[:, :], jrow[:, :], AF.Exp)
        c.barrier([sc], [t], f"{name}_b1")
        s_mm = c.sem(f"{name}_mm")
        s_dv = c.sem(f"{name}_dv")
        for ch in range(nch):
            pb = ps[:, (ch % 2) * JW:(ch % 2) * JW + JW]
            if ch >= 2:
                s_dv.wait_at(t, ch - 1)
            s_mm.inc(t.matmul(pb, colv_row[:, ch * 128:(ch + 1) * 128],
                              inv2[:, :], start=True, stop=True))
            s_mm.wait(v)
            v.tensor_scalar(red[:, :], pb, 1.0 / TWO_PI, 0.5, OP.mult, OP.add)
            v.drain()
            v.tensor_copy(qi_[:, :], red[:, :])
            v.drain()
            v.tensor_copy(red[:, :], qi_[:, :])
            v.drain()
            v.scalar_tensor_tensor(red[:, :], red[:, :], -TWO_PI,
                                   pb, OP.mult, OP.add)
            v.drain()
            cnd = qi_[:, :].bitcast(F32)
            v.tensor_scalar(cnd, red[:, :], -math.pi, None, OP.is_lt)
            v.drain()
            s_dv.inc(v.scalar_tensor_tensor(red[:, :], cnd, TWO_PI,
                                            red[:, :], OP.mult, OP.add))
            c.barrier([v], [sc], f"{name}_b2{ch}")
            sc.activation(sinb[:, ch * JW:(ch + 1) * JW], red[:, :], AF.Sin)
            sc.drain()
            sc.activation(sh[:, :], red[:, :], AF.Sin, scale=0.5)
            sc.drain()
            sc.activation(sh[:, :], sh[:, :], AF.Square)
            c.barrier([sc], [v], f"{name}_b3{ch}")
            v.tensor_scalar(cosb[:, ch * JW:(ch + 1) * JW], sh[:, :], -2.0, 1.0,
                            OP.mult, OP.add)
            c.barrier([v], [v, t, sc], f"{name}_b4{ch}")
        c.barrier([v, sc], engs(c, ALL), f"{name}_fin")


def emit_rope_fm(c, tiles, cosb, sinb, JW, ident, name="rf"):
    nc, g, sy, sc, v, t = c.nc, c.g, c.sy, c.sc, c.v, c.t
    with contextlib.ExitStack() as les:
        ps_t = c.ps(f"{name}_ps", [128, 2048], BF16, les)
        tm = c.sb(f"{name}_tm", [128, 128], BF16, les)
        A = c.sb(f"{name}_A", [128, JW], BF16, les)
        Bt = c.sb(f"{name}_B", [128, JW], BF16, les)
        s_tp = c.sem(f"{name}_tp")
        s_tc = c.sem(f"{name}_tc")
        s_rp = c.sem(f"{name}_rp")
        for idx, (fm, ch) in enumerate(tiles):
            off = (idx % 2) * 1024
            if idx >= 1:
                s_rp.wait_at(t, 2 * idx)
            s_tp.inc(t.transpose(ps_t[:, off:off + 128], fm, ident[:, :]))
            s_tp.wait(v)
            v.tensor_copy(tm[:, :], ps_t[:, off:off + 128])
            v.drain()
            x1 = tm[:, 0:JW]
            x2 = tm[:, JW:2 * JW]
            cs = cosb[:, ch * JW:(ch + 1) * JW]
            sn = sinb[:, ch * JW:(ch + 1) * JW]
            v.tensor_tensor(A[:, :], x1, sn, OP.mult)
            v.tensor_tensor(Bt[:, :], x2, sn, OP.mult)
            v.drain()
            v.tensor_tensor(x1, x1, cs, OP.mult)
            v.tensor_tensor(x2, x2, cs, OP.mult)
            v.drain()
            v.tensor_tensor(x1, x1, Bt[:, :], OP.subtract)
            v.drain()
            s_rp.inc(v.tensor_tensor(x2, x2, A[:, :], OP.add))
            s_rp.wait(t)
            s_tp.inc(t.transpose(ps_t[:, off + 128:off + 256], tm[:, :],
                                 ident[:, :]))
            s_tp.wait(v)
            s_rp.inc(v.tensor_copy(fm, ps_t[:, off + 128:off + 256]))
        c.barrier([v], engs(c, ALL), f"{name}_fin")


def emit_rope_apply(c, x1, x2, cos_ap, sin_ap, W, name="ra", P=64):
    nc, v = c.nc, c.v
    with contextlib.ExitStack() as les:
        A = c.sb(f"{name}_A", [P, W], BF16, les)
        B = c.sb(f"{name}_B", [P, W], BF16, les)
        v.tensor_tensor(A[:, :], x1, sin_ap, OP.mult)
        v.tensor_tensor(B[:, :], x2, sin_ap, OP.mult)
        v.drain()
        v.tensor_tensor(x1, x1, cos_ap, OP.mult)
        v.tensor_tensor(x2, x2, cos_ap, OP.mult)
        v.drain()
        v.tensor_tensor(x1, x1, B[:, :], OP.subtract)
        v.tensor_tensor(x2, x2, A[:, :], OP.add)
        c.barrier([v], engs(c, ALL), f"{name}_b")



def build_attn_b(D=2048, T=2048, TH=1024, H=16, DH=128, NG=4, c=None, io=None):
    nd = D // 128
    nh = TH // 128
    nk = T // 128
    c, es, io, standalone = _stage(c, io, [LN_EPS])
    nc = c.nc
    g, sy, sc, v, t = c.g, c.sy, c.sc, c.v, c.t
    if standalone:
        io["x"] = nc.dram_tensor("x", [D, T], BF16, kind="ExternalInput").ap()
        io["wq"] = nc.dram_tensor("wq", [D, D], F32, kind="ExternalInput").ap()
        io["wk"] = nc.dram_tensor("wk", [D, D], F32, kind="ExternalInput").ap()
        io["wv"] = nc.dram_tensor("wv", [D, D], F32, kind="ExternalInput").ap()
        io["wo"] = nc.dram_tensor("wo", [D, D], F32, kind="ExternalInput").ap()
        io["rowv"] = nc.dram_tensor("rowv", [128, nh], F32, kind="ExternalInput").ap()
        io["colv"] = nc.dram_tensor("colv", [1, T], F32, kind="ExternalInput").ap()
        io["xr"] = nc.dram_tensor("xr", [D, TH], BF16, kind="ExternalInput").ap()
        io["xout"] = nc.dram_tensor("xout", [D, TH], BF16, kind="ExternalOutput").ap()
    x, wq, wk, wv, wo = io["x"], io["wq"], io["wk"], io["wv"], io["wo"]
    rowv_h, colv_h, xr, xout = io["rowv"], io["colv"], io["xr"], io["xout"]
    with es:
        att = c.sb("att", [128, (D // 128) * TH], BF16)
        colb = c.sb("colb", [128, T], F32)
        rowv = c.sb("rowv_s", [128, nh], F32)
        cosb = c.sb("cosb", [128, (T // 128) * 64], BF16)
        sinb = c.sb("sinb", [128, (T // 128) * 64], BF16)
        ident = emit_identity(c, es)
        s_b = c.sem("meta", "h")
        s_b.dinc(sy.dma_start(rowv[:, :], rowv_h[:, :]))
        s_b.wait(sc)
        s_b.wait(v)
        with contextlib.ExitStack() as les_c:
            colv = c.sb("colv_s", [1, T], F32, les_c)
            s_c = c.sem("cl", "h")
            s_c.dinc(sy.dma_start(colv[:, :], colv_h[:, :]))
            s_c.wait(v)
            s_c.wait(t)
            emit_bcast_row(c, colv[:, :], colb, T, "cb")

            def _freq_ab(jrow, v_, sc_, c_):
                v_.tensor_scalar(jrow[:, :].bitcast(I32), jrow[:, :].bitcast(I32),
                                 31, None, OP.bitwise_and)
                v_.drain()
                v_.tensor_copy(jrow[:, :], jrow[:, :].bitcast(I32))
                v_.drain()
                v_.tensor_scalar(jrow[:, :], jrow[:, :],
                                 -4.0 * math.log(10000.0) / 128.0, None, OP.mult)
                v_.drain()
            emit_rope_tables_tm(c, es, colv[:, :], T, 64, _freq_ab,
                                cosb, sinb)
        with contextlib.ExitStack() as les_u:
            u = c.sb("u", [128, nd * T], BF16, les_u)
            with contextlib.ExitStack() as les_l:
                m_b, r_b = emit_ln_stats(c, les_l, x, D, T)
                emit_u(c, x, u, D, 0, T, m_b, r_b)
            for hg in range(H // NG):
                with contextlib.ExitStack() as les_g:
                    qg_ = c.sb(f"qg{hg}", [128, NG * TH], BF16, les_g)
                    kg_ = c.sb(f"kg{hg}", [128, NG * T], BF16, les_g)
                    vtr_g = c.sb(f"vt{hg}", [128, NG * T], BF16, les_g)
                    r0 = hg * NG * 128
                    r1 = (hg + 1) * NG * 128
                    emit_proj(c, u, T, wq[r0:r1, :], nd, NG, qg_, TH,
                              name=f"q{hg}")
                    emit_proj(c, u, T, wk[r0:r1, :], nd, NG, kg_, T,
                              name=f"k{hg}")
                    with contextlib.ExitStack() as les_v:
                        vsb = c.sb(f"vs{hg}", [128, NG * T], BF16, les_v)
                        emit_proj(c, u, T, wv[r0:r1, :], nd, NG, vsb, T,
                                  name=f"v{hg}")
                        emit_transposes(
                            c, [(vsb[:, oc * T + kc * 128:oc * T + kc * 128 + 128],
                                 vtr_g[:, (oc * nk + kc) * 128:(oc * nk + kc) * 128 + 128])
                                for oc in range(NG) for kc in range(nk)],
                            ident, f"vt{hg}")
                    tiles = []
                    for j in range(NG):
                        for tc in range(TH // 128):
                            tiles.append((qg_[:, j * TH + tc * 128:j * TH + tc * 128 + 128], tc))
                        for tc in range(T // 128):
                            tiles.append((kg_[:, j * T + tc * 128:j * T + tc * 128 + 128], tc))
                    emit_rope_fm(c, tiles, cosb, sinb, 64, ident,
                                 name=f"rp{hg}")
                    emit_attn_core(c, qg_, kg_, vtr_g, att, colb, rowv, ident,
                                   NG, 1, TH, T, att_h0=hg * NG, name=f"at{hg}")
        ysb = c.sb("ysb", [128, nd * TH], BF16)
        emit_proj(c, att, TH, wo, nd, nd, ysb, TH, name="o")
        emit_residual_out(c, xr, ysb, nd, TH, xout, xr_dt=BF16)
    return nc



def emit_dequant(c, rw_hbm, scl_hbm, out_hbm, nch, W, m7, m8, s16i, name,
                 WSUB=2048):
    nc, g, sy, sc, v, t = c.nc, c.g, c.sy, c.sc, c.v, c.t
    WSUB = min(W, WSUB)
    tasks = [(j, wo, min(W - wo, WSUB)) for j in range(nch)
             for wo in range(0, W, WSUB)]
    with contextlib.ExitStack() as les:
        rw = c.sb(f"{name}_rw", [128, 2 * WSUB], I32, les)
        scl = c.sb(f"{name}_scl", [1, 2 * WSUB], F32, les)
        sclb = c.sb(f"{name}_sclb", [128, WSUB], F32, les)
        onescol = c.sb(f"{name}_one", [1, 128], F32, les)
        ps = c.ps(f"{name}_ps", [128, 1024], F32, les)
        mag = c.sb(f"{name}_mag", [128, WSUB], I32, les)
        magf = c.sb(f"{name}_magf", [128, WSUB], F32, les)
        sgn = c.sb(f"{name}_sgn", [128, WSUB], I32, les)
        bits = c.sb(f"{name}_bits", [128, WSUB], I32, les)
        fsub = c.sb(f"{name}_fsub", [128, WSUB], F32, les)
        cond = c.sb(f"{name}_cond", [128, WSUB], I32, les)
        vmag = c.sb(f"{name}_vmag", [128, WSUB], F32, les)
        outt = c.sb(f"{name}_out", [128, 2 * WSUB], BF16, les)
        s_ld = [c.sem(f"{name}_l0", "h"), c.sem(f"{name}_l1", "h")]
        s_sl = [c.sem(f"{name}_m0", "g"), c.sem(f"{name}_m1", "g")]
        s_bc = c.sem(f"{name}_bc")
        s_bcc = c.sem(f"{name}_bcc")
        s_dq = c.sem(f"{name}_dq")
        s_st = [c.sem(f"{name}_s0", "g"), c.sem(f"{name}_s1", "g")]
        v.memset(onescol[:, :], 1.0)
        c.barrier([v], [t], f"{name}_b0")
        for n_, (j, wo, WW) in enumerate(tasks):
            sl = n_ % 2
            ngb = cdiv(WW, 512)
            if n_ >= 2:
                s_dq.wait_at(sy, n_ - 1)
                s_dq.wait_at(g, n_ - 1)
            s_ld[sl].dinc(sy.dma_start(rw[:, sl * WSUB:sl * WSUB + WW],
                                       rw_hbm[j * 128:(j + 1) * 128, wo:wo + WW]))
            s_sl[sl].dinc(g.dma_start(scl[:, sl * WSUB:sl * WSUB + WW],
                                      scl_hbm[j:j + 1, wo:wo + WW]))
            s_sl[sl].wait(t)
            if n_ >= 1:
                s_dq.wait_at(t, n_)
            for gi in range(ngb):
                t0, t1 = gi * 512, min(WW, (gi + 1) * 512)
                pb = ps[:, (gi % 2) * 512:(gi % 2) * 512 + (t1 - t0)]
                if gi >= 2:
                    t.wait_ge(s_bcc.h, s_bcc.n - 1)
                s_bc.inc(t.matmul(pb, onescol[:, :],
                                  scl[:, sl * WSUB + t0:sl * WSUB + t1],
                                  start=True, stop=True))
                s_bc.wait(v)
                s_bcc.inc(v.tensor_copy(sclb[:, t0:t1], pb))
            v.drain()
            s_ld[sl].wait(v)
            rwj = rw[:, sl * WSUB:sl * WSUB + WW]
            W_ = slice(0, WW)
            v.tensor_scalar(mag[:, W_], rwj, m7[:, 0:1], None, OP.bitwise_and)
            v.tensor_scalar(sgn[:, W_], rwj, m8[:, 0:1], None, OP.bitwise_and)
            v.drain()
            v.tensor_scalar(magf[:, W_], mag[:, W_], s16i[:, 0:1], 0.25, OP.mult, OP.add)
            v.tensor_scalar(sgn[:, W_], sgn[:, W_], 0, None, OP.is_gt)
            v.drain()
            v.tensor_copy(mag[:, W_], magf[:, W_])
            v.drain()
            v.tensor_scalar(bits[:, W_], mag[:, W_], 252, None, OP.add)
            v.drain()
            v.tensor_scalar(bits[:, W_], bits[:, W_], 22, None,
                            OP.logical_shift_left)
            v.tensor_scalar(cond[:, W_], mag[:, W_], 2, None, OP.is_lt)
            v.tensor_scalar(sgn[:, W_], sgn[:, W_], 31, 0x3F800000,
                            OP.logical_shift_left, OP.bitwise_or)
            v.drain()
            f = bits[:, W_].bitcast(F32)
            v.tensor_scalar(fsub[:, W_], f, 2.0, 1.0, OP.mult, OP.subtract)
            v.drain()
            v.select(vmag[:, W_], cond[:, W_], fsub[:, W_], f, add_drain=True)
            v.drain()
            v.tensor_tensor(fsub[:, W_], sgn[:, W_].bitcast(F32), sclb[:, W_],
                            OP.mult)
            v.drain()
            oj = outt[:, sl * WSUB:sl * WSUB + WW]
            if n_ >= 2:
                s_st[sl].wait(v)
            s_dq.inc(v.tensor_tensor(oj, vmag[:, W_], fsub[:, W_], OP.mult))
            g.wait_ge(s_dq.h, s_dq.n)
            s_st[sl].dinc(g.dma_start(out_hbm[j * 128:(j + 1) * 128, wo:wo + WW], oj))
        g.wait_ge(s_st[0].h, s_st[0].n)
        g.wait_ge(s_st[1].h, s_st[1].n)
        c.barrier(engs(c, ALL), engs(c, ALL), f"{name}_fin")


def build_dequant(nch_gu=11, W_gu=2048, nch_dn=2, W_dn=5632, c=None, io=None):
    c, es, io, standalone = _stage(c, io, [])
    nc = c.nc
    g, sy, sc, v, t = c.g, c.sy, c.sc, c.v, c.t
    if standalone:
        io["rgu"] = nc.dram_tensor("rgu", [nch_gu * 128, W_gu], I32, kind="ExternalInput").ap()
        io["sgu"] = nc.dram_tensor("sgu", [nch_gu, W_gu], mybir.dt.float16, kind="ExternalInput").ap()
        io["rdn"] = nc.dram_tensor("rdn", [nch_dn * 128, W_dn], I32, kind="ExternalInput").ap()
        io["sdn"] = nc.dram_tensor("sdn", [nch_dn, W_dn], mybir.dt.float16, kind="ExternalInput").ap()
        io["dqgu"] = nc.dram_tensor("dqgu", [nch_gu * 128, W_gu], BF16, kind="ExternalOutput").ap()
        io["dqdn"] = nc.dram_tensor("dqdn", [nch_dn * 128, W_dn], BF16, kind="ExternalOutput").ap()
    rgu, sgu, rdn, sdn = io["rgu"], io["sgu"], io["rdn"], io["sdn"]
    dqgu, dqdn = io["dqgu"], io["dqdn"]
    with es:
        m78 = c.sb("m78", [128, 2], I32)
        s16i = c.sb("s16i", [128, 1], F32)
        pidx = c.sb("pidx", [128, 1], I32)
        kf = c.sb("kf", [128, 2], I32)
        s0 = c.sem("s0")
        g.iota(pidx[:, :], [[0, 1]], channel_multiplier=1).then_inc(s0.h, 1)
        s0.n += 1
        s0.wait(v)
        v.tensor_scalar(pidx[:, :], pidx[:, :], 7, None, OP.bitwise_and)
        v.drain()
        v.tensor_scalar(kf[:, 0:1], pidx[:, :], 2, None, OP.logical_shift_left)
        v.drain()
        v.tensor_scalar(kf[:, 0:1], kf[:, 0:1], -1, None, OP.bitwise_xor)
        v.drain()
        v.tensor_scalar(kf[:, 0:1], kf[:, 0:1], 128, None, OP.add)
        v.drain()
        v.tensor_scalar(s16i[:, :].bitcast(I32), kf[:, 0:1], 23, None,
                        OP.logical_shift_left)
        v.drain()
        v.tensor_scalar(kf[:, 0:1], pidx[:, :], 2, None, OP.logical_shift_left)
        v.drain()
        v.tensor_scalar(kf[:, 0:1], kf[:, 0:1], 127, None, OP.add)
        v.drain()
        v.tensor_scalar(kf[:, 1:2], kf[:, 0:1], 23, None, OP.logical_shift_left)
        v.drain()
        v.tensor_scalar(kf[:, 0:1].bitcast(F32), kf[:, 1:2].bitcast(F32),
                        7.0, None, OP.mult)
        v.drain()
        v.tensor_copy(m78[:, 0:1], kf[:, 0:1].bitcast(F32))
        v.drain()
        v.tensor_scalar(kf[:, 0:1], m78[:, 0:1], 1, None, OP.logical_shift_left)
        v.tensor_scalar(kf[:, 1:2], m78[:, 0:1], -1, None, OP.bitwise_xor)
        v.drain()
        v.tensor_tensor(m78[:, 1:2], kf[:, 0:1], kf[:, 1:2], OP.bitwise_and)
        c.barrier([v], engs(c, ALL), "b_sh")
        emit_dequant(c, rgu, sgu, dqgu, nch_gu, W_gu, m78[:, 0:1], m78[:, 1:2],
                     s16i[:, 0:1], "gu")
        emit_dequant(c, rdn, sdn, dqdn, nch_dn, W_dn, m78[:, 0:1], m78[:, 1:2],
                     s16i[:, 0:1], "dn")
    return nc



def build_ffn(D=2048, TH=1024, I=5632, c=None, io=None):
    nd = D // 128
    ni = I // 128
    c, es, io, standalone = _stage(c, io, [LN_EPS, 1e-6])
    nc = c.nc
    g, sy, sc, v, t = c.g, c.sy, c.sc, c.v, c.t
    if standalone:
        io["x"] = nc.dram_tensor("x", [D, TH], BF16, kind="ExternalInput").ap()
        io["wgu"] = nc.dram_tensor("wgu", [2 * ni * 128, nd * 128], BF16, kind="ExternalInput").ap()
        io["wdn"] = nc.dram_tensor("wdn", [nd * 128, ni * 128], BF16, kind="ExternalInput").ap()
        io["xout"] = nc.dram_tensor("xout", [D, TH], BF16, kind="ExternalOutput").ap()
    x, wgu, wdn, xout = io["x"], io["wgu"], io["wdn"], io["xout"]
    ng = cdiv(TH, 512)
    with es:
        h = c.sb("h", [128, ni * TH], BF16)
        y = c.sb("y", [128, nd * TH], BF16)
        with contextlib.ExitStack() as les_u:
            m_b, r_b = emit_ln_stats(c, les_u, x, D, TH)
            u = c.sb("u", [128, nd * TH], BF16, les_u)
            emit_u(c, x, u, D, 0, TH, m_b, r_b)
            with contextlib.ExitStack() as les_g:
                wrot = c.sb("gw", [128, 4 * nd * 128], BF16, les_g)
                psum = c.ps("gps", [128, 2048], F32, les_g)
                gt = c.sb("gt", [128, 512], F32, les_g)
                s_w = [c.sem("gw0", "h"), c.sem("gw1", "h"), c.sem("gw2", "h"), c.sem("gw3", "h")]
                s_mm = c.sem("gmm")
                s_ev = c.sem("gev")
                s_h = c.sem("gh")
                for j in range(ni):
                    slg, slu = (j % 2) * 2, (j % 2) * 2 + 1
                    wg = wrot[:, slg * nd * 128:(slg + 1) * nd * 128]
                    wu = wrot[:, slu * nd * 128:(slu + 1) * nd * 128]
                    if j >= 2:
                        s_mm.wait_at(sy, (j - 1) * ng)
                    s_w[slg].dinc(sy.dma_start(wg, wgu[j * 128:(j + 1) * 128, :]))
                    s_w[slu].dinc(sy.dma_start(wu, wgu[(ni + j) * 128:(ni + j + 1) * 128, :]))
                    s_w[slg].wait(t)
                    s_w[slu].wait(t)
                    for gi in range(ng):
                        t0, t1 = gi * 512, min(TH, (gi + 1) * 512)
                        pg = psum[:, gi * 512:gi * 512 + (t1 - t0)]
                        pu = psum[:, 1024 + gi * 512:1024 + gi * 512 + (t1 - t0)]
                        if j >= 1:
                            s_h.wait_at(t, (j - 1) * ng + gi + 1)
                        for i in range(nd):
                            t.matmul(pg, wg[:, i * 128:(i + 1) * 128],
                                     u[:, i * TH + t0:i * TH + t1],
                                     start=(i == 0), stop=(i == nd - 1))
                        for i in range(nd):
                            mm = t.matmul(pu, wu[:, i * 128:(i + 1) * 128],
                                          u[:, i * TH + t0:i * TH + t1],
                                          start=(i == 0), stop=(i == nd - 1))
                        s_mm.inc(mm)
                        s_mm.wait(sc)
                        if j >= 1 or gi >= 1:
                            sc.wait_ge(s_h.h, s_h.n)
                        s_ev.inc(sc.activation(gt[:, 0:t1 - t0], pg, AF.Silu))
                        s_ev.wait(v)
                        s_h.inc(v.tensor_tensor(h[:, j * TH + t0:j * TH + t1],
                                                gt[:, 0:t1 - t0], pu, OP.mult))
                c.barrier([v], engs(c, ALL), "b_h")
        emit_proj(c, h, TH, wdn, ni, nd, y, TH, name="dn", cast=False)
        with contextlib.ExitStack() as les_r:
            ones = c.sb("r_ones", [128, 128], BF16, les_r)
            ysq = c.sb("r_ysq", [128, 2 * TH], BF16, les_r)
            ps_q = c.ps("r_psq", [128, TH], F32, les_r)
            rms = c.sb("r_rms", [128, TH], F32, les_r)
            s_mm = c.sem("r_mm")
            s_sq = c.sem("r_sq")
            v.memset(ones[:, :], 1.0)
            v.drain()
            v.sem_inc(s_mm.h, 1)
            s_mm.n += 1
            s_mm.wait_at(t, 1)
            for i in range(nd):
                if i >= 2:
                    s_mm.wait_at(sc, i)
                sq = ysq[:, (i % 2) * TH:(i % 2) * TH + TH]
                s_sq.inc(sc.activation(sq, y[:, i * TH:(i + 1) * TH], AF.Square))
                t.wait_ge(s_sq.h, s_sq.n)
                mm2 = None
                for gi in range(ng):
                    t0, t1 = gi * 512, min(TH, (gi + 1) * 512)
                    mm2 = t.matmul(ps_q[:, t0:t1], ones[:, :], sq[:, t0:t1],
                                   start=(i == 0), stop=(i == nd - 1))
                s_mm.inc(mm2)
            s_mm.wait(sc)
            sc.mul(rms[:, :], ps_q[:, :], 1.0 / D)
            sc.drain()
            sc.activation(rms[:, :], rms[:, :], AF.Ln, bias=1e-6)
            sc.drain()
            sc.activation(rms[:, :], rms[:, :], AF.Exp, scale=-0.5)
            c.barrier([sc], [v], "b_rms")
            for i in range(nd):
                v.tensor_tensor(y[:, i * TH:(i + 1) * TH], y[:, i * TH:(i + 1) * TH],
                                rms[:, :], OP.mult)
            c.barrier([v], engs(c, ALL), "b_y")
        emit_residual_out(c, x, y, nd, TH, xout, xr_dt=BF16)
    return nc



MOE_HID = [1024, 2048, 3072, 4096, 1024, 2048, 3072, 4096]
MOE_COST_LAMBDA = 1e-7


def build_moe(D=2048, TH=1024, L=1024, out_dt=BF16, c=None, io=None):
    nd = D // 128
    nl = L // 128
    ne = 8
    ng = cdiv(TH, 512)
    nh = TH // 128
    c, es, io, standalone = _stage(c, io, [LN_EPS])
    nc = c.nc
    g, sy, sc, v, t = c.g, c.sy, c.sc, c.v, c.t
    HIDS = MOE_HID + [L]
    if standalone:
        io["x"] = nc.dram_tensor("x", [D, TH], BF16, kind="ExternalInput").ap()
        io["wdn"] = nc.dram_tensor("wdn", [L, nd * 128], F32, kind="ExternalInput").ap()
        io["bdn"] = nc.dram_tensor("bdn", [128, nl], F32, kind="ExternalInput").ap()
        io["wrt"] = nc.dram_tensor("wrt", [128, (D // 128) * ne], F32, kind="ExternalInput").ap()
        io["brt"] = nc.dram_tensor("brt", [ne, 1], F32, kind="ExternalInput").ap()
        io["wup"] = nc.dram_tensor("wup", [nd * 128, L], F32, kind="ExternalInput").ap()
        io["bup"] = nc.dram_tensor("bup", [128, nd], F32, kind="ExternalInput").ap()
        io["wcr"] = nc.dram_tensor("wcr", [D, nd * 128], F32, kind="ExternalInput").ap()
        io["bcr"] = nc.dram_tensor("bcr", [128, nd], F32, kind="ExternalInput").ap()
        for e, he in enumerate(HIDS):
            io[f"w1_{e}"] = nc.dram_tensor(f"w1_{e}", [he, nl * 128], F32, kind="ExternalInput").ap()
            io[f"b1_{e}"] = nc.dram_tensor(f"b1_{e}", [128, he // 128], F32, kind="ExternalInput").ap()
            io[f"w2_{e}"] = nc.dram_tensor(f"w2_{e}", [L, (he // 128) * 128], F32, kind="ExternalInput").ap()
            io[f"b2_{e}"] = nc.dram_tensor(f"b2_{e}", [128, nl], F32, kind="ExternalInput").ap()
        io["xout"] = nc.dram_tensor("xout", [D, TH], out_dt, kind="ExternalOutput").ap()
    x, wdn, bdn, wrt, brt = io["x"], io["wdn"], io["bdn"], io["wrt"], io["brt"]
    wup, bup, wcr, bcr, xout = io["wup"], io["bup"], io["wcr"], io["bcr"], io["xout"]
    w1s = [io[f"w1_{e}"] for e in range(9)]
    b1s = [io[f"b1_{e}"] for e in range(9)]
    w2s = [io[f"w2_{e}"] for e in range(9)]
    b2s = [io[f"b2_{e}"] for e in range(9)]

    with es:
        identF = c.sb("identF", [128, 128], F32)
        onesrow = c.sb("onesrow", [1, 128], F32)
        sel8 = c.sb("sel8", [8, 8 * 128], F32)
        ident = emit_identity(c, es)
        v.tensor_copy(identF[:, :], ident[:, :])
        v.memset(onesrow[:, :], 1.0)
        c.barrier([v], engs(c, ALL), "bidf")
        s_z = c.sem("z")
        g.iota(sel8[:, :].bitcast(I32), [[-1, 8], [0, 128]],
               channel_multiplier=1).then_inc(s_z.h, 1)
        s_z.n += 1
        s_z.wait(v)
        v.tensor_scalar(sel8[:, :].bitcast(I32), sel8[:, :].bitcast(I32), 0,
                        None, OP.is_equal)
        v.drain()
        v.tensor_copy(sel8[:, :], sel8[:, :].bitcast(I32))
        c.barrier([v], engs(c, ALL), "bsel")

        hg = c.sb("hg", [128, nl * TH], BF16)
        yacc = c.sb("yacc", [128, nl * TH], F32)
        osum = c.sb("osum", [128, nd * TH], BF16)
        wT8 = c.sb("wT8", [8, TH], F32)
        addv = c.sb("addv", [128, nl], F32)
        v.memset(yacc[:, :], 0.0)
        v.memset(addv[:, :], 0.0)
        c.barrier([v], engs(c, ALL), "bz")

        with contextlib.ExitStack() as les_u:
            u = c.sb("u", [128, nd * TH], BF16, les_u)
            with contextlib.ExitStack() as les_l:
                m_b, r_b = emit_ln_stats(c, les_l, x, D, TH)
                emit_u(c, x, u, D, 0, TH, m_b, r_b)

            with contextlib.ExitStack() as les_r:
                lg = c.sb("lg", [8, TH], F32, les_r)
                wr = c.sb("wr", [128, nd * ne], BF16, les_r)
                rb = c.sb("rb", [ne, 1], F32, les_r)
                cost = c.sb("cost", [ne, 1], F32, les_r)
                ps_r = c.ps("ps_r", [8, TH], F32, les_r)
                lgt = c.sb("lgt", [128, nh * 8], F32, les_r)
                wmap = c.sb("wmap", [128, nh * 8], F32, les_r)
                v1 = c.sb("v1", [128, 4], F32, les_r)
                ps_tr = c.ps("ps_tr", [128, 1024], F32, les_r)
                s_w = c.sem("rw", "g")
                s_wh = c.sem("rwh", "h")
                s_mm = c.sem("rmm")
                s_w.dinc(g.dma_start(wr[:, :], wrt[:, :]))
                s_wh.dinc(sy.dma_start(rb[:, :], brt[:, :]))
                s_wh.wait(sc)
                g.iota(cost[:, :].bitcast(I32), [[0, 1]],
                       channel_multiplier=1).then_inc(s_z.h, 1)
                s_z.n += 1
                s_z.wait(v)
                v.tensor_scalar(cost[:, :].bitcast(I32), cost[:, :].bitcast(I32),
                                3, None, OP.bitwise_and)
                v.drain()
                v.tensor_copy(cost[:, :], cost[:, :].bitcast(I32))
                v.drain()
                lam = MOE_COST_LAMBDA * 2048 * 1024
                v.tensor_scalar(cost[:, :], cost[:, :], lam, lam, OP.mult, OP.add)
                c.barrier([v], [sc, t], "bcost")
                s_w.wait(t)
                for gi in range(ng):
                    t0, t1 = gi * 512, min(TH, (gi + 1) * 512)
                    for i in range(nd):
                        mm = t.matmul(ps_r[:, t0:t1], wr[:, i * ne:(i + 1) * ne],
                                      u[:, i * TH + t0:i * TH + t1],
                                      start=(i == 0), stop=(i == nd - 1))
                    s_mm.inc(mm)
                s_mm.wait(sc)
                sc.activation(lg[:, :], ps_r[:, :], AF.Identity, bias=rb[:, :])
                c.barrier([sc], [v], "blg")
                v.tensor_scalar(lg[:, :], lg[:, :], cost[:, 0:1], None, OP.subtract)
                c.barrier([v], [t], "blg2")
                s_tp = c.sem("rtp")
                s_tc = c.sem("rtc")
                for ch in range(nh):
                    off = (ch % 2) * 512
                    if ch >= 2:
                        s_tc.wait_at(t, ch - 1)
                    s_tp.inc(t.transpose(ps_tr[:, off:off + 8],
                                         lg[:, ch * 128:(ch + 1) * 128],
                                         identF[0:8, 0:8]))
                    s_tp.wait(v)
                    s_tc.inc(v.tensor_copy(lgt[:, ch * 8:(ch + 1) * 8],
                                           ps_tr[:, off:off + 8]))
                c.barrier([v], [v, sc, t], "brt2")
                s_vg = c.sem("vg")
                s_gs = c.sem("gs")
                for ch in range(nh):
                    lgc = lgt[:, ch * 8:(ch + 1) * 8]
                    wmc = wmap[:, ch * 8:(ch + 1) * 8]
                    v.tensor_reduce(v1[:, 0:1], lgc, AX.X, OP.max)
                    v.drain()
                    v.tensor_scalar(wmc, lgc, v1[:, 0:1], None, OP.is_ge)
                    v.drain()
                    v.scalar_tensor_tensor(wmc, wmc, NEG, lgc, OP.mult, OP.add)
                    v.drain()
                    v.tensor_reduce(v1[:, 1:2], wmc, AX.X, OP.max)
                    v.drain()
                    v.drain()
                    s_vg.inc(v.tensor_tensor(v1[:, 2:3], v1[:, 0:1], v1[:, 1:2],
                                             OP.subtract))
                    s_vg.wait(sc)
                    s_gs.inc(sc.activation(v1[:, 2:3], v1[:, 2:3], AF.Sigmoid))
                    s_gs.wait(v)
                    v.tensor_scalar(v1[:, 3:4], v1[:, 2:3], -1.0, 1.0,
                                    OP.mult, OP.add)
                    v.drain()
                    v.tensor_scalar(wmc, wmc, v1[:, 1:2], None, OP.is_ge)
                    v.drain()
                    v.tensor_scalar(wmc, wmc, v1[:, 3:4], None, OP.mult)
                    v.drain()
                    v.tensor_scalar(lgc, lgc, v1[:, 0:1], None, OP.is_ge)
                    v.drain()
                    v.tensor_scalar(lgc, lgc, v1[:, 2:3], None, OP.mult)
                    v.drain()
                    v.tensor_tensor(wmc, wmc, lgc, OP.add)
                    v.drain()
                c.barrier([v], [t, v, sc], "bwm")
                s_tp2 = c.sem("rtp2")
                s_tc2 = c.sem("rtc2")
                for ch in range(nh):
                    off = (ch % 2) * 512
                    if ch >= 2:
                        s_tc2.wait_at(t, ch - 1)
                    s_tp2.inc(t.transpose(ps_tr[0:8, off:off + 128],
                                          wmap[:, ch * 8:(ch + 1) * 8],
                                          identF[:, :]))
                    s_tp2.wait(v)
                    s_tc2.inc(v.tensor_copy(wT8[0:8, ch * 128:(ch + 1) * 128],
                                            ps_tr[0:8, off:off + 128]))
                c.barrier([v], engs(c, ALL), "brt3")

            with contextlib.ExitStack() as les_b:
                ball = c.sb("ball", [128, nl + nd], F32, les_b)
                s_b = c.sem("mb", "h")
                s_b.dinc(sy.dma_start(ball[:, 0:nl], bdn[:, :]))
                s_b.dinc(sy.dma_start(ball[:, nl:nl + nd], bcr[:, :]))
                s_b.wait(sc)
                emit_proj(c, u, TH, wdn, nd, nl, hg, TH, name="dn",
                          bias_sb=ball, bias_off=0, act=AF.Gelu)
                with contextlib.ExitStack() as les_cg:
                    ug = c.sb("ug", [128, nd * TH], BF16, les_cg)
                    for i in range(nd):
                        sc.activation(ug[:, i * TH:(i + 1) * TH],
                                      u[:, i * TH:(i + 1) * TH], AF.Gelu)
                    c.barrier([sc], engs(c, ALL), "bug")
                    emit_proj(c, ug, TH, wcr, nd, nd, osum, TH, name="cr",
                              bias_sb=ball, bias_off=nl)

        for e, he in enumerate(HIDS):
            nhe = he // 128
            with contextlib.ExitStack() as les_e:
                u1g = c.sb(f"u1g{e}", [128, nhe * TH], BF16, les_e)
                b1t = c.sb(f"b1t{e}", [128, nhe], F32, les_e)
                gb1 = c.sb(f"gb1{e}", [128, nhe], F32, les_e)
                gb1b = c.sb(f"gb1b{e}", [128, nhe], BF16, les_e)
                web = c.sb(f"web{e}", [128, TH], F32, les_e)
                ps_b = c.ps(f"psb{e}", [128, 1024], F32, les_e)
                s_b1 = c.sem("eb", "h")
                s_b1.dinc(sy.dma_start(b1t[:, :], b1s[e][:, :]))
                s_b1.wait(sc)
                sc.activation(gb1[:, :], b1t[:, :], AF.Gelu)
                sc.drain()
                sc.copy(gb1b[:, :], gb1[:, :])
                c.barrier([sc], engs(c, ALL), f"be{e}")
                if e < ne:
                    s_wb = c.sem("wb")
                    for gi in range(ng):
                        t0, t1 = gi * 512, min(TH, (gi + 1) * 512)
                        if gi >= 1:
                            s_wb.wait_at(t, 2 * gi)
                        pb = ps_b[:, (gi % 2) * 512:(gi % 2) * 512 + (t1 - t0)]
                        s_wb.inc(t.matmul(pb, sel8[:, e * 128:(e + 1) * 128],
                                          wT8[0:8, t0:t1], start=True, stop=True))
                        s_wb.wait(v)
                        s_wb.inc(v.tensor_copy(web[:, t0:t1], pb))
                    c.barrier([v], engs(c, ALL), f"bw{e}")
                emit_proj(c, hg, TH, w1s[e], nl, nhe, u1g, TH,
                          name=f"e1_{e}", bias_sb=b1t, bias_off=0, act=AF.Gelu)
                for j in range(nhe):
                    sl_ = u1g[:, j * TH:(j + 1) * TH]
                    if e < ne:
                        v.tensor_scalar(sl_, sl_, gb1[:, j:j + 1], None,
                                        OP.subtract)
                        v.drain()
                        v.tensor_tensor(sl_, sl_, web[:, :], OP.mult)
                        v.drain()
                    else:
                        v.tensor_scalar(sl_, sl_, 0.1, None, OP.mult)
                        v.drain()
                c.barrier([v], engs(c, ALL), f"bs{e}")
                with contextlib.ExitStack() as les_w2:
                    wrot = c.sb(f"w2r{e}", [128, 2 * nhe * 128], BF16, les_w2)
                    ps_y = c.ps(f"psy{e}", [128, 1024], F32, les_w2)
                    ps_ce = c.ps(f"psce{e}", [128, nl], F32, les_w2)
                    s_w2 = [c.sem("w20", "g"), c.sem("w21", "g")]
                    s_m2 = c.sem("m2")
                    s_e2 = c.sem("e2")
                    ipj = ng + (1 if e < ne else 0)
                    for j in range(nl):
                        wj = wrot[:, (j % 2) * nhe * 128:(j % 2) * nhe * 128 + nhe * 128]
                        if j >= 2:
                            s_m2.wait_at(g, (j - 1) * ipj)
                        s_w2[j % 2].dinc(g.dma_start(wj, w2s[e][j * 128:(j + 1) * 128, :]))
                        s_w2[j % 2].wait(t)
                        for gi in range(ng):
                            t0, t1 = gi * 512, min(TH, (gi + 1) * 512)
                            pb = ps_y[:, (gi % 2) * 512:(gi % 2) * 512 + (t1 - t0)]
                            if j * ng + gi >= 2:
                                s_e2.wait_at(t, (j * ng + gi) - 1)
                            for i in range(nhe):
                                mm = t.matmul(pb, wj[:, i * 128:(i + 1) * 128],
                                              u1g[:, i * TH + t0:i * TH + t1],
                                              start=(i == 0), stop=(i == nhe - 1))
                            s_m2.inc(mm)
                            s_m2.wait(v)
                            dst = yacc[:, j * TH + t0:j * TH + t1]
                            ev = v.tensor_tensor(dst, pb, dst, OP.add)
                            v.drain()
                            s_e2.inc(ev)
                        if e < ne:
                            for i in range(nhe):
                                mm = t.matmul(ps_ce[:, j:j + 1],
                                              wj[:, i * 128:(i + 1) * 128],
                                              gb1b[:, i:i + 1],
                                              start=(i == 0), stop=(i == nhe - 1))
                            s_m2.inc(mm)
                    s_m2.wait(v)
                    if e < ne:
                        v.tensor_tensor(addv[:, :], ps_ce[:, :], addv[:, :], OP.add)
                        v.drain()
                    c.barrier([v], engs(c, ALL), f"bf{e}")

        with contextlib.ExitStack() as les_b2:
            b2t = c.sb("b2t", [128, nl], F32, les_b2)
            s_b2 = c.sem("b2", "h")
            for e in range(ne + 1):
                s_b2.dinc(sy.dma_start(b2t[:, :], b2s[e][:, :]))
                s_b2.wait(v)
                wgt = 1.0 if e < ne else 0.1
                v.scalar_tensor_tensor(addv[:, :], b2t[:, :], wgt,
                                       addv[:, :], OP.mult, OP.add)
                c.barrier([v], [sy, v], f"bb2{e}")
        with contextlib.ExitStack() as les_f:
            ybf = c.sb("ybf", [128, nl * TH], BF16, les_f)
            for j in range(nl):
                v.tensor_scalar(ybf[:, j * TH:(j + 1) * TH],
                                yacc[:, j * TH:(j + 1) * TH],
                                addv[:, j:j + 1], None, OP.add)
            c.barrier([v], engs(c, ALL), "bfin")
            with contextlib.ExitStack() as les_bb:
                ball2 = c.sb("ball2", [128, nd], F32, les_bb)
                s_bb = c.sem("bb", "h")
                s_bb.dinc(sy.dma_start(ball2[:, :], bup[:, :]))
                s_bb.wait(v)
                emit_proj(c, ybf, TH, wup, nl, nd, osum, TH, name="up",
                          bias_sb=ball2, bias_off=0, accum_sb=osum)
        emit_residual_out(c, x, osum, nd, TH, xout, out_dt=out_dt, xr_dt=BF16)
    return nc



def build_mamba1(D=2048, TH=1024, E=4096, NST=16, DTR=128, DCONV=4,
                 c=None, io=None, ar=None, pad_zero=False):
    nd = D // 128
    nE = E // 128
    TE = TH + DCONV - 1
    ng = cdiv(TE, 512)
    ngz = TH // 512
    c, es, io, standalone = _stage(c, io, [LN_EPS, 1.0])
    nc = c.nc
    g, sy, sc, v, t = c.g, c.sy, c.sc, c.v, c.t
    if standalone:
        io["x"] = nc.dram_tensor("x", [D, TE], BF16, kind="ExternalInput").ap()
        io["win"] = nc.dram_tensor("win", [2 * E, nd * 128], F32, kind="ExternalInput").ap()
        io["cw"] = nc.dram_tensor("cw", [128, nE * DCONV], F32, kind="ExternalInput").ap()
        io["cb"] = nc.dram_tensor("cb", [128, nE], F32, kind="ExternalInput").ap()
        io["wxd"] = nc.dram_tensor("wxd", [128, nE * 128], F32, kind="ExternalInput").ap()
        io["wxbc"] = nc.dram_tensor("wxbc", [128, nE * 32], F32, kind="ExternalInput").ap()
        io["wdt"] = nc.dram_tensor("wdt", [E, DTR], F32, kind="ExternalInput").ap()
        io["bdt"] = nc.dram_tensor("bdt", [128, nE], F32, kind="ExternalInput").ap()
        io["xc_s"] = nc.dram_tensor("xc_s", [E, TH], BF16, kind="ExternalOutput").ap()
        io["dl_s"] = nc.dram_tensor("dl_s", [E, TH], BF16, kind="ExternalOutput").ap()
        io["z_s"] = nc.dram_tensor("z_s", [E, TH], BF16, kind="ExternalOutput").ap()
        io["bc_s"] = nc.dram_tensor("bc_s", [2 * NST, TH], F32, kind="ExternalOutput").ap()
    x, win, cw, cb = io["x"], io["win"], io["cw"], io["cb"]
    wxd, wxbc, wdt, bdt = io["wxd"], io["wxbc"], io["wdt"], io["bdt"]
    xc_s, dl_s, z_s, bc_s = io["xc_s"], io["dl_s"], io["z_s"], io["bc_s"]

    with es:
        xc = c.sb("xc", [128, nE * TH], BF16)
        cwt = c.sb("cwt", [128, nE * DCONV], F32)
        cbt = c.sb("cbt", [128, nE], F32)
        bdtt = c.sb("bdtt", [128, nE], F32)
        s_m = c.sem("m1", "h")
        s_m.dinc(sy.dma_start(cwt[:, :], cw[:, :]))
        s_m.dinc(sy.dma_start(cbt[:, :], cb[:, :]))
        s_m.dinc(sy.dma_start(bdtt[:, :], bdt[:, :]))
        s_m.wait(sc)
        s_m.wait(v)
        with contextlib.ExitStack() as les_u:
            u = c.sb("u", [128, nd * TE], BF16, les_u)
            if pad_zero:
                pad = DCONV - 1
                for i in range(nd):
                    v.memset(u[:, i * TE:i * TE + pad], 0.0)
                c.barrier([v], engs(c, ALL), "m1up")
                with contextlib.ExitStack() as les_l:
                    m_b, r_b = emit_ln_stats(c, les_l, x[:, pad:TE], D, TH)
                    emit_u(c, x[:, pad:TE], u, D, 0, TH, m_b, r_b,
                           ustride=TE, uoff=pad)
            else:
                with contextlib.ExitStack() as les_l:
                    m_b, r_b = emit_ln_stats(c, les_l, x, D, TE)
                    emit_u(c, x, u, D, 0, TE, m_b, r_b)
            with contextlib.ExitStack() as les_p:
                wrot = c.sb("m1w", [128, 2 * nd * 128], BF16, les_p)
                psum = c.ps("m1ps", [128, 2 * 512], F32, les_p)
                xpj = c.sb("xpj", [128, 2 * TE], F32, les_p)
                cvacc = c.sb("cvacc", [128, TH], F32, les_p)
                s_w = [c.sem("m1w0", "g"), c.sem("m1w1", "g")]
                s_mm = c.sem("m1mm")
                s_ev = c.sem("m1ev")
                s_cv = c.sem("m1cv")
                nmm = 0
                for j in range(nE):
                    wj = wrot[:, (j % 2) * nd * 128:(j % 2) * nd * 128 + nd * 128]
                    if j >= 2:
                        s_mm.wait_at(g, (j - 1) * ng)
                    s_w[j % 2].dinc(g.dma_start(wj, win[j * 128:(j + 1) * 128, :]))
                    s_w[j % 2].wait(t)
                    xp = xpj[:, (j % 2) * TE:(j % 2) * TE + TE]
                    for gi in range(ng):
                        t0, t1 = gi * 512, min(TE, (gi + 1) * 512)
                        pb = psum[:, (nmm % 2) * 512:(nmm % 2) * 512 + (t1 - t0)]
                        if nmm >= 2:
                            s_ev.wait_at(t, nmm - 1)
                        for i in range(nd):
                            mm = t.matmul(pb, wj[:, i * 128:(i + 1) * 128],
                                          u[:, i * TE + t0:i * TE + t1],
                                          start=(i == 0), stop=(i == nd - 1))
                        s_mm.inc(mm)
                        nmm += 1
                        s_mm.wait(sc)
                        if j >= 2 and gi == 0:
                            s_cv.wait_at(sc, 2 * (j - 1))
                        s_ev.inc(sc.copy(xp[:, t0:t1], pb))
                    s_ev.wait(v)
                    if j >= 1:
                        s_cv.wait_at(v, 2 * j)
                    v.tensor_scalar(cvacc[:, :], xp[:, 0:TH],
                                    cwt[:, j * DCONV:j * DCONV + 1], None, OP.mult)
                    v.drain()
                    for k in range(1, DCONV):
                        v.scalar_tensor_tensor(
                            cvacc[:, :], xp[:, k:k + TH],
                            cwt[:, j * DCONV + k:j * DCONV + k + 1],
                            cvacc[:, :], OP.mult, OP.add)
                        v.drain()
                    v.sem_inc(s_cv.h, 1)
                    s_cv.n += 1
                    s_cv.wait(sc)
                    s_cv.inc(sc.activation(xc[:, j * TH:(j + 1) * TH], cvacc[:, :],
                                           AF.Silu, bias=cbt[:, j:j + 1]))
                c.barrier([sc, v], engs(c, ALL), "m1bx")
            s_st = c.sem("m1st", "g")
            for j in range(nE):
                s_st.dinc(g.dma_start(xc_s[j * 128:(j + 1) * 128, :],
                                      xc[:, j * TH:(j + 1) * TH]))
            with contextlib.ExitStack() as les_z:
                wrot = c.sb("m1zw", [128, 2 * nd * 128], BF16, les_z)
                psum = c.ps("m1zps", [128, 2 * 512], F32, les_z)
                zt = c.sb("zt", [128, 2 * TH], BF16, les_z)
                s_w = [c.sem("mzw0", "g"), c.sem("mzw1", "g")]
                s_mm = c.sem("mzmm")
                s_ev = c.sem("mzev")
                s_sz = [c.sem("mzst0", "h"), c.sem("mzst1", "h")]
                nmm = 0
                for j in range(nE):
                    wj = wrot[:, (j % 2) * nd * 128:(j % 2) * nd * 128 + nd * 128]
                    if j >= 2:
                        s_mm.wait_at(g, (j - 1) * ngz)
                    s_w[j % 2].dinc(g.dma_start(wj, win[(nE + j) * 128:(nE + j + 1) * 128, :]))
                    s_w[j % 2].wait(t)
                    zj = zt[:, (j % 2) * TH:(j % 2) * TH + TH]
                    for gi in range(ngz):
                        t0 = DCONV - 1 + gi * 512
                        t1 = t0 + 512
                        pb = psum[:, (nmm % 2) * 512:(nmm % 2) * 512 + 512]
                        if nmm >= 2:
                            s_ev.wait_at(t, nmm - 1)
                        for i in range(nd):
                            mm = t.matmul(pb, wj[:, i * 128:(i + 1) * 128],
                                          u[:, i * TE + t0:i * TE + t1],
                                          start=(i == 0), stop=(i == nd - 1))
                        s_mm.inc(mm)
                        nmm += 1
                        s_mm.wait(sc)
                        if j >= 2 and gi == 0:
                            s_sz[j % 2].wait(sc)
                        s_ev.inc(sc.copy(zj[:, gi * 512:gi * 512 + 512], pb))
                    s_ev.wait(sy)
                    s_sz[j % 2].dinc(sy.dma_start(z_s[j * 128:(j + 1) * 128, :], zj))
                sy.wait_ge(s_sz[0].h, s_sz[0].n)
                sy.wait_ge(s_sz[1].h, s_sz[1].n)
                c.barrier([sc, v], engs(c, ALL), "m1bz")
        with contextlib.ExitStack() as les_d:
            dt_t = c.sb("dt_t", [128, TH], BF16, les_d)
            wxt = c.sb("wxt", [128, nE * 128], BF16, les_d)
            wxbt = c.sb("wxbt", [128, nE * 32], BF16, les_d)
            psum = c.ps("dps", [128, 2 * 512], F32, les_d)
            psbc = c.ps("dpsbc", [32, 2 * 512], F32, les_d)
            bct = c.sb("bct", [32, TH], F32, les_d)
            s_w = c.sem("dw", "g")
            s_mm = c.sem("dmm")
            s_ev = c.sem("dev")
            s_e2 = c.sem("dev2")
            s_w.dinc(g.dma_start(wxt[:, :], wxd[:, :]))
            s_w.dinc(g.dma_start(wxbt[:, :], wxbc[:, :]))
            s_w.wait(t)
            for gi in range(ngz):
                t0, t1 = gi * 512, gi * 512 + 512
                pb = psum[:, (gi % 2) * 512:(gi % 2) * 512 + 512]
                pbc = psbc[:, (gi % 2) * 512:(gi % 2) * 512 + 512]
                if gi >= 2:
                    s_ev.wait_at(t, gi - 1)
                    s_e2.wait_at(t, gi - 1)
                for i in range(nE):
                    mm = t.matmul(pb, wxt[:, i * 128:(i + 1) * 128],
                                  xc[:, i * TH + t0:i * TH + t1],
                                  start=(i == 0), stop=(i == nE - 1))
                s_mm.inc(mm)
                for i in range(nE):
                    mm = t.matmul(pbc,
                                  wxbt[:, i * 32:(i + 1) * 32],
                                  xc[:, i * TH + t0:i * TH + t1],
                                  start=(i == 0), stop=(i == nE - 1))
                s_mm.inc(mm)
                s_mm.wait(sc)
                s_ev.inc(sc.copy(dt_t[:, t0:t1], pb))
                s_mm.wait(v)
                s_e2.inc(v.tensor_copy(bct[:, t0:t1], pbc))
            c.barrier([v, sc], engs(c, ALL), "m1bd")
            if ar is not None:
                s_put = c.sem("arp", "g")
                s_put.dinc(g.dma_start(ar["loc"][0:DTR, :], dt_t[:, :]))
                s_put.dinc(g.dma_start(ar["loc"][DTR:DTR + 2 * NST, :], bct[:, :]))
                s_put.wait(g)
                s_cc = emit_cc(c, "AllReduce", [ar["loc"]], [ar["full"]],
                               ar["groups"], op=OP.add)
                s_cc.wait(g)
                s_get = c.sem("arg", "g")
                s_get.dinc(g.dma_start(dt_t[:, :], ar["full"][0:DTR, :]))
                s_get.dinc(g.dma_start(bct[:, :], ar["full"][DTR:DTR + 2 * NST, :]))
                s_get.wait(g)
                c.barrier([g], engs(c, ALL), "m1ar")
            s_sb = c.sem("m1sb", "h")
            s_sb.dinc(sy.dma_start(bc_s[:, :], bct[:, :]))
            with contextlib.ExitStack() as les_dd:
                wrot = c.sb("ddw", [128, 2 * 128], BF16, les_dd)
                dps = c.ps("ddps", [128, 2 * 512], F32, les_dd)
                sp1 = c.sb("sp1", [128, TH], F32, les_dd)
                sp2 = c.sb("sp2", [128, TH], F32, les_dd)
                dlt = c.sb("dlt", [128, 2 * TH], BF16, les_dd)
                s_w2 = [c.sem("ddw0", "g"), c.sem("ddw1", "g")]
                s_m2 = c.sem("ddmm")
                s_e2 = c.sem("ddev")
                s_e2v = c.sem("ddevv")
                s_s2 = [c.sem("ddst0", "h"), c.sem("ddst1", "h")]
                nmm = 0
                for j in range(nE):
                    wj = wrot[:, (j % 2) * 128:(j % 2) * 128 + 128]
                    if j >= 2:
                        s_m2.wait_at(g, (j - 1) * ngz)
                    s_w2[j % 2].dinc(g.dma_start(wj, wdt[j * 128:(j + 1) * 128, :]))
                    s_w2[j % 2].wait(t)
                    dj = dlt[:, (j % 2) * TH:(j % 2) * TH + TH]
                    for gi in range(ngz):
                        t0, t1 = gi * 512, gi * 512 + 512
                        pb = dps[:, (nmm % 2) * 512:(nmm % 2) * 512 + 512]
                        if nmm >= 2:
                            s_e2.wait_at(t, nmm - 1)
                        mm = t.matmul(pb, wj[:, :], dt_t[:, t0:t1],
                                      start=True, stop=True)
                        s_m2.inc(mm)
                        s_m2.wait(sc)
                        if nmm >= 2:
                            s_e2v.wait_at(sc, nmm - 1)
                        sc.activation(sp1[:, t0:t1], pb, AF.Abs,
                                      bias=bdtt[:, j:j + 1])
                        sc.drain()
                        sc.activation(sp1[:, t0:t1], sp1[:, t0:t1], AF.Exp,
                                      scale=-1.0)
                        sc.drain()
                        sc.activation(sp1[:, t0:t1], sp1[:, t0:t1], AF.Ln,
                                      bias=1.0)
                        sc.drain()
                        s_e2.inc(sc.activation(sp2[:, t0:t1], pb, AF.Relu,
                                               bias=bdtt[:, j:j + 1]))
                        s_e2.wait(v)
                        if j >= 2 and gi == 0:
                            s_s2[j % 2].wait(v)
                        s_e2v.inc(v.tensor_tensor(dj[:, t0:t1], sp1[:, t0:t1],
                                                  sp2[:, t0:t1], OP.add))
                        nmm += 1
                    s_e2v.wait(sy)
                    s_s2[j % 2].dinc(sy.dma_start(dl_s[j * 128:(j + 1) * 128, :], dj))
                sy.wait_ge(s_s2[0].h, s_s2[0].n)
                sy.wait_ge(s_s2[1].h, s_s2[1].n)
            sy.wait_ge(s_sb.h, s_sb.n)
            g.wait_ge(s_st.h, s_st.n)
            c.barrier(engs(c, ALL), engs(c, ALL), "m1fin")
    return nc


def build_mamba2(EH=2048, T=2048, NST=16, c=None, io=None):
    nch = EH // 128
    nsub = 128 // (128 // NST)
    cpt = 128 // NST
    ntile = nch * NST
    ng = cdiv(T, 512)
    c, es, io, standalone = _stage(c, io, [1.0])
    nc = c.nc
    g, sy, sc, v, t = c.g, c.sy, c.sc, c.v, c.t
    if standalone:
        io["xc"] = nc.dram_tensor("xc", [EH, T], BF16, kind="ExternalInput").ap()
        io["dl"] = nc.dram_tensor("dl", [EH, T], BF16, kind="ExternalInput").ap()
        io["z"] = nc.dram_tensor("z", [EH, T], BF16, kind="ExternalInput").ap()
        io["bc"] = nc.dram_tensor("bc", [2 * NST, T], F32, kind="ExternalInput").ap()
        io["a_t"] = nc.dram_tensor("a_t", [128, ntile], F32, kind="ExternalInput").ap()
        io["dp"] = nc.dram_tensor("dp", [128, nch], F32, kind="ExternalInput").ap()
        io["y_s"] = nc.dram_tensor("y_s", [EH, T], BF16, kind="ExternalOutput").ap()
    xc, dl, z, bc = io["xc"], io["dl"], io["z"], io["bc"]
    a_t, dp, y_s = io["a_t"], io["dp"], io["y_s"]

    with es:
        Pj = c.sb("Pj", [128, NST * 128], BF16)
        Qj = c.sb("Qj", [128, NST * 128], BF16)
        PB = c.sb("PB", [32, 128], BF16)
        PC = c.sb("PC", [32, 128], BF16)
        grid = c.sb("grid", [128, 128], I32)
        gf = c.sb("gf", [128, 128], F32)
        prow = c.sb("prow", [128, 2], F32)
        s0 = c.sem("p0")
        g.iota(grid[:, :], [[1, 128]], channel_multiplier=0).then_inc(s0.h, 1)
        s0.n += 1
        g.iota(prow[:, 0:1].bitcast(I32), [[0, 1]], channel_multiplier=1).then_inc(s0.h, 1)
        s0.n += 1
        s0.wait(v)
        v.tensor_scalar(prow[:, 1:2].bitcast(I32), prow[:, 0:1].bitcast(I32),
                        4, None, OP.logical_shift_right)
        v.drain()
        v.tensor_copy(gf[:, 0:2], prow[:, 0:2].bitcast(I32))
        v.drain()
        v.tensor_copy(prow[:, 0:2], gf[:, 0:2])
        v.drain()
        v.tensor_scalar(grid[:, :], grid[:, :], 4, None, OP.logical_shift_right)
        v.drain()
        v.tensor_copy(gf[:, :], grid[:, :])
        v.drain()
        v.tensor_scalar(gf[:, :], gf[:, :], prow[:, 0:1], None, OP.subtract)
        v.drain()
        for j in range(NST):
            v.tensor_scalar(Pj[:, j * 128:(j + 1) * 128], gf[:, :],
                            -8.0 * j, None, OP.is_equal)
        v.drain()
        c.barrier([v], [g], "bq0")
        g.iota(grid[:, :], [[1, 128]], channel_multiplier=0).then_inc(s0.h, 1)
        s0.n += 1
        s0.wait(v)
        v.tensor_copy(gf[:, :], grid[:, :])
        v.drain()
        v.tensor_scalar(gf[:, :], gf[:, :], prow[:, 1:2], None, OP.subtract)
        v.drain()
        for j in range(NST):
            v.tensor_scalar(Qj[:, j * 128:(j + 1) * 128], gf[:, :],
                            8.0 * j, None, OP.is_equal)
        v.drain()
        v.tensor_scalar(grid[0:32, :], grid[0:32, :], 15, None, OP.bitwise_and)
        v.drain()
        v.tensor_copy(gf[0:32, :], grid[0:32, :])
        v.drain()
        v.tensor_scalar(gf[0:32, :], gf[0:32, :], prow[0:32, 0:1], None,
                        OP.subtract)
        v.drain()
        v.tensor_scalar(PB[:, :], gf[0:32, :], 0.0, None, OP.is_equal)
        v.tensor_scalar(PC[:, :], gf[0:32, :], -16.0, None, OP.is_equal)
        c.barrier([v], engs(c, ALL), "bpat")

        Brep = c.sb("Brep", [128, T], BF16)
        Crep = c.sb("Crep", [128, T], BF16)
        bct = c.sb("bct", [32, T], F32)
        s_m = c.sem("m2m", "h")
        s_m.dinc(sy.dma_start(bct[:, :], bc[:, :]))
        s_m.wait(v)
        with contextlib.ExitStack() as les_br:
            bcb = c.sb("bcb", [32, T], BF16, les_br)
            ps_r = c.ps("ps_r", [128, 1024], F32, les_br)
            s_r = c.sem("m2r")
            s_rc = c.sem("m2rc")
            v.tensor_copy(bcb[:, :], bct[:, :])
            c.barrier([v], engs(c, ALL), "bbc")
            for gi in range(ng):
                t0, t1 = gi * 512, min(T, (gi + 1) * 512)
                pb = ps_r[:, (gi % 2) * 512:(gi % 2) * 512 + (t1 - t0)]
                if gi >= 1:
                    s_rc.wait_at(t, 2 * gi)
                s_r.inc(t.matmul(pb, PB[:, :], bcb[:, t0:t1], start=True, stop=True))
                s_r.wait(v)
                s_rc.inc(v.tensor_copy(Brep[:, t0:t1], pb))
                s_rc.wait(t)
                s_r.inc(t.matmul(pb, PC[:, :], bcb[:, t0:t1], start=True, stop=True))
                s_r.wait(v)
                s_rc.inc(v.tensor_copy(Crep[:, t0:t1], pb))
                s_rc.wait(t)
            c.barrier([v], engs(c, ALL), "bbr")

        at = c.sb("at", [128, ntile], F32)
        dpt = c.sb("dpt", [128, nch], F32)
        s_a = c.sem("m2a", "h")
        s_a.dinc(sy.dma_start(at[:, :], a_t[:, :]))
        s_a.dinc(sy.dma_start(dpt[:, :], dp[:, :]))
        s_a.wait(sc)
        sc.activation(at[:, :], at[:, :], AF.Exp)
        c.barrier([sc], [v], "ba1")
        v.tensor_scalar(at[:, :], at[:, :], -1.0, None, OP.mult)
        c.barrier([v], engs(c, ALL), "ba2")

        with contextlib.ExitStack() as les_m:
            xcc = c.sb("xcc", [128, 2 * T], BF16, les_m)
            dlc = c.sb("dlc", [128, 2 * T], BF16, les_m)
            dxc = c.sb("dxc", [128, T], BF16, les_m)
            dA = c.sb("dA", [128, T], BF16, les_m)
            dB = c.sb("dB", [128, T], BF16, les_m)
            hS = c.sb("hS", [128, T], BF16, les_m)
            hC = c.sb("hC", [128, T], BF16, les_m)
            zc = c.sb("zc", [128, 2 * T], BF16, les_m)
            yc = c.sb("yc", [128, 2 * T], BF16, les_m)
            zero = c.sb("zero", [128, 1], F32, les_m)
            ps_rep = c.ps("ps_rep", [128, T], F32, les_m)
            ps_y = c.ps("ps_y", [128, T], F32, les_m)
            s_ld = [c.sem("ml0", "h"), c.sem("ml1", "h")]
            s_zl = [c.sem("mz0", "g"), c.sem("mz1", "g")]
            s_dx = c.sem("mdx")
            s_rp = c.sem("mrp")
            s_da = c.sem("mda")
            s_sc2 = c.sem("msc")
            s_ym = c.sem("mym")
            s_ye = c.sem("mye")
            s_st = [c.sem("ms0", "g"), c.sem("ms1", "g")]
            v.memset(zero[:, :], 0.0)
            c.barrier([v], engs(c, ALL), "bz0")
            for ch in range(nch):
                sl = ch % 2
                xck = xcc[:, sl * T:sl * T + T]
                dlk = dlc[:, sl * T:sl * T + T]
                if ch >= 2:
                    s_st[sl].wait(sy)
                s_ld[sl].dinc(sy.dma_start(xck, xc[ch * 128:(ch + 1) * 128, :]))
                s_ld[sl].dinc(sy.dma_start(dlk, dl[ch * 128:(ch + 1) * 128, :]))
                s_ld[sl].wait(v)
                if ch >= 1:
                    s_rp.wait_at(v, 2 * NST * ch)
                s_dx.inc(v.tensor_tensor(dxc[:, :], dlk, xck, OP.mult))
                v.drain()
                if ch >= 2:
                    s_ye.wait_at(g, 3 * (ch - 1))
                s_zl[sl].dinc(g.dma_start(zc[:, sl * T:sl * T + T],
                                          z[ch * 128:(ch + 1) * 128, :]))
                for j in range(NST):
                    tile = ch * NST + j
                    s_dx.wait(t)
                    if j == 0 and ch >= 1:
                        s_ye.wait_at(t, 3 * ch - 1)
                    if tile >= 1:
                        s_da.wait_at(t, 2 * tile)
                    for gi in range(ng):
                        t0, t1 = gi * 512, min(T, (gi + 1) * 512)
                        mm = t.matmul(ps_rep[:, t0:t1], Pj[:, j * 128:(j + 1) * 128],
                                      dlk[:, t0:t1], start=True, stop=True)
                    s_rp.inc(mm)
                    s_rp.wait(v)
                    v.tensor_scalar(dA[:, :], ps_rep[:, :], at[:, tile:tile + 1],
                                    None, OP.mult)
                    v.drain()
                    v.sem_inc(s_da.h, 1)
                    s_da.n += 1
                    s_da.wait(sc)
                    s_sc2.inc(sc.activation(dA[:, :], dA[:, :], AF.Exp))
                    s_sc2.wait(t)
                    for gi in range(ng):
                        t0, t1 = gi * 512, min(T, (gi + 1) * 512)
                        mm = t.matmul(ps_rep[:, t0:t1], Pj[:, j * 128:(j + 1) * 128],
                                      dxc[:, t0:t1], start=True, stop=True)
                    s_rp.inc(mm)
                    s_rp.wait(v)
                    s_da.inc(v.tensor_tensor(dB[:, :], ps_rep[:, :], Brep[:, :],
                                             OP.mult))
                    v.drain()
                    s_sc2.wait(v)
                    v.tensor_tensor_scan(hS[:, :], dA[:, :], dB[:, :], zero[:, 0:1],
                                         OP.mult, OP.add)
                    v.drain()
                    s_ym.wait(v)
                    v.tensor_tensor(hC[:, :], hS[:, :], Crep[:, :], OP.mult)
                    v.drain()
                    v.sem_inc(s_da.h, 1)
                    s_da.n += 1
                    s_da.wait(t)
                    for gi in range(ng):
                        t0, t1 = gi * 512, min(T, (gi + 1) * 512)
                        mm = t.matmul(ps_y[:, t0:t1], Qj[:, j * 128:(j + 1) * 128],
                                      hC[:, t0:t1], start=(j == 0), stop=(j == NST - 1))
                    s_ym.inc(mm)
                s_ym.wait(sc)
                s_zl[sl].wait(sc)
                yk = yc[:, sl * T:sl * T + T]
                if ch >= 2:
                    s_st[sl].wait(sc)
                s_ye.inc(sc.activation(yk, zc[:, sl * T:sl * T + T], AF.Silu))
                s_ym.wait(v)
                s_ye.wait(v)
                s_ye.inc(v.scalar_tensor_tensor(hC[:, :], xck, dpt[:, ch:ch + 1],
                                                ps_y[:, :], OP.mult, OP.add))
                v.drain()
                s_ye.inc(v.tensor_tensor(yk, yk, hC[:, :], OP.mult))
                g.wait_ge(s_ye.h, s_ye.n)
                s_st[sl].dinc(g.dma_start(y_s[ch * 128:(ch + 1) * 128, :], yk))
            g.wait_ge(s_st[0].h, s_st[0].n)
            g.wait_ge(s_st[1].h, s_st[1].n)
            c.barrier(engs(c, ALL), engs(c, ALL), "m2fin")
    return nc


def build_mamba3(D=2048, TH=1024, E=4096, c=None, io=None):
    nd = D // 128
    nE = E // 128
    c, es, io, standalone = _stage(c, io, [])
    nc = c.nc
    if standalone:
        io["y"] = nc.dram_tensor("y", [E, TH], BF16, kind="ExternalInput").ap()
        io["wout"] = nc.dram_tensor("wout", [D, nE * 128], F32, kind="ExternalInput").ap()
        io["x"] = nc.dram_tensor("x", [D, TH], BF16, kind="ExternalInput").ap()
        io["xout"] = nc.dram_tensor("xout", [D, TH], BF16, kind="ExternalOutput").ap()
    y, wout, x, xout = io["y"], io["wout"], io["x"], io["xout"]
    with es:
        yt = c.sb("yt", [128, nE * TH], BF16)
        osb = c.sb("osb", [128, nd * TH], BF16)
        s_ld = c.sem("ld", "h")
        for j in range(nE):
            s_ld.dinc(c.sy.dma_start(yt[:, j * TH:(j + 1) * TH],
                                     y[j * 128:(j + 1) * 128, :]))
        s_ld.wait(c.t)
        emit_proj(c, yt, TH, wout, nE, nd, osb, TH, name="wo")
        emit_residual_out(c, x, osb, nd, TH, xout, xr_dt=BF16)
    return nc



def build_hh2(T=2048, HEAVY=128, RECENT=128, BLK=128, c=None, io=None):
    nchk = T // 128
    START = HEAVY + RECENT
    c, es, io, standalone = _stage(c, io, [])
    nc = c.nc
    g, sy, sc, v, t = c.g, c.sy, c.sc, c.v, c.t
    if standalone:
        io["ps0"] = nc.dram_tensor("ps0", [T // 2, T], BF16, kind="ExternalInput").ap()
        io["ps1"] = nc.dram_tensor("ps1", [T // 2, T], BF16, kind="ExternalInput").ap()
        io["maskT"] = nc.dram_tensor("maskT", [128, nchk * T], BF16, kind="ExternalOutput").ap()
    ps0, ps1, maskT_h = io["ps0"], io["ps1"], io["maskT"]

    with es:
        probsT = c.sb("probsT", [128, nchk * T], BF16)
        maskT = c.sb("maskT_s", [128, nchk * T], BF16)
        acc = c.sb("acc", [128, nchk], F32)
        tot = c.sb("tot", [128, nchk], BF16)
        toti = c.sb("toti", [128, nchk], F32)
        amsel = c.sb("amsel", [128, nchk], F32)
        am = c.sb("am", [128, nchk], F32)
        accp = c.sb("accp", [128, nchk], F32)
        cand = c.sb("cand", [128, nchk], F32)
        drop = c.sb("drop", [128, nchk], F32)
        pmin = c.sb("pmin", [128, 1], F32)
        mval = c.sb("mval", [1, 1], F32)
        identF = c.sb("identF", [128, 128], F32)
        inft = c.sb("inft", [128, nchk], F32)
        onescol = c.sb("onescol", [1, 128], F32)
        bmB = c.sb("bmB", [128, nchk * BLK], F32)
        rmB = c.sb("rmB", [128, nchk * BLK], BF16)
        vals = c.sb("vals", [128, nchk * BLK], I32)
        ps_m = c.ps("ps_m", [128, 512], F32)
        ps_pm = c.ps("ps_pm", [128, 512], F32)

        s_ld = c.sem("hld", "h")
        for ch in range(nchk):
            s_ld.dinc(sy.dma_start(probsT[:, ch * T:ch * T + T // 2],
                                   ps0[:, ch * 128:(ch + 1) * 128], transpose=True))
            s_ld.dinc(sy.dma_start(probsT[:, ch * T + T // 2:(ch + 1) * T],
                                   ps1[:, ch * 128:(ch + 1) * 128], transpose=True))
        identB0 = emit_identity(c, es, name="ident_h2")
        v.tensor_copy(identF[:, :], identB0[:, :])
        v.memset(inft[:, :], 1e30)
        v.memset(onescol[:, :], 1.0)
        v.memset(tot[:, 0:2], 1.0)
        v.memset(tot[:, 2:nchk], 0.0)
        v.memset(toti[:, 0:2], 0.0)
        v.memset(toti[:, 2:nchk], 1e30)
        v.memset(maskT[:, :], 0.0)
        v.drain()
        for ch in range(2):
            v.memset(maskT[:, ch * T:ch * T + START], 1.0)
        v.drain()
        v.memset(acc[:, :], 0.0)
        v.drain()
        s_ld.wait(v)
        for ch in range(2):
            v.tensor_reduce(acc[:, ch:ch + 1], probsT[:, ch * T:ch * T + START],
                            AX.X, OP.add)
        c.barrier([v], engs(c, ALL), "hini")

        mview = maskT[:].rearrange("p (c t) -> p t c", c=nchk)
        pview = probsT[:].rearrange("p (c t) -> p t c", c=nchk)
        s_mh = c.sem("mh")
        s_mb = c.sem("mb")
        s_blk = c.sem("blk")
        for t0 in range(START, T, BLK):
            nti = min(BLK, T - t0)
            if t0 > START:
                s_mh.wait_at(g, 3 * (t0 - START))
            g.iota(vals[:, :], [[128, nchk], [-1, BLK]], base=-t0,
                   channel_multiplier=1).then_inc(s_blk.h, 1)
            s_blk.n += 1
            s_blk.wait(v)
            v.tensor_scalar(bmB[:, :], vals[:, :], -(RECENT + 1), None, OP.is_gt)
            v.drain()
            v.tensor_scalar(bmB[:, :], bmB[:, :], 1e30, None, OP.mult)
            v.tensor_scalar(rmB[:, :], vals[:, :], -RECENT, None, OP.is_ge)
            v.drain()
            v.tensor_scalar(cand[:, :], vals[:, 0:nchk * BLK:BLK], 0, None,
                            OP.is_le) if False else None
            v.tensor_scalar(vals[:, :], vals[:, :], 0, None, OP.is_le)
            v.drain()
            v.tensor_tensor(rmB[:, :], rmB[:, :], vals[:, :].bitcast(F32)) \
                if False else None
            v.tensor_tensor(rmB[:, :], rmB[:, :], vals[:, :], OP.mult)
            v.drain()
            v.sem_inc(s_blk.h, 1)
            s_blk.n += 1
            bview = bmB[:].rearrange("p (c t) -> p t c", c=nchk)
            rview = rmB[:].rearrange("p (c t) -> p t c", c=nchk)
            for ti in range(nti):
                tcur = t0 + ti
                v.tensor_tensor(amsel[:, :], acc[:, :], toti[:, :], OP.max)
                v.drain()
                v.tensor_tensor(am[:, :], amsel[:, :], bview[:, ti, :], OP.add)
                v.drain()
                s_mh.inc(v.tensor_reduce(pmin[:, :], am[:, :], AX.X, OP.min))
                v.tensor_tensor(accp[:, :], acc[:, :], pview[:, tcur, :], OP.add)
                v.tensor_scalar(cand[:, :], am[:, :], 1e29, None, OP.is_lt)
                v.drain()
                s_mh.wait(t)
                s_mb.inc(t.transpose(ps_pm[0:1, 0:128], pmin[:, :],
                                     identF[:, :]))
                s_mb.wait(v)
                v.tensor_reduce(mval[:, :], ps_pm[0:1, 0:128], AX.X, OP.min)
                v.drain()
                v.sem_inc(s_mh.h, 1)
                s_mh.n += 1
                s_mh.wait(t)
                s_mb.inc(t.matmul(ps_m[:, 0:1], onescol[:, :],
                                  mval[:, :], start=True, stop=True))
                s_mb.wait(v)
                v.tensor_scalar(drop[:, :], am[:, :], ps_m[:, 0:1], None,
                                OP.is_equal)
                v.drain()
                if tcur == START:
                    v.tensor_copy(mview[:, tcur, :], cand[:, :])
                else:
                    v.tensor_tensor(mview[:, tcur, :], cand[:, :], drop[:, :],
                                    OP.subtract)
                v.drain()
                v.tensor_tensor(tot[:, :], mview[:, tcur, :], rview[:, ti, :],
                                OP.max)
                v.drain()
                v.tensor_tensor(acc[:, :], accp[:, :], tot[:, :], OP.mult)
                s_mh.inc(v.tensor_scalar(toti[:, :], tot[:, :], -1e30, 1e30,
                                         OP.mult, OP.add))
                v.drain()
        s_mh.wait(g)
        s_st = c.sem("hst", "g")
        for ch in range(nchk):
            s_st.dinc(g.dma_start(maskT_h[:, ch * T:(ch + 1) * T],
                                  maskT[:, ch * T:(ch + 1) * T]))
        g.wait_ge(s_st.h, s_st.n)
        c.barrier(engs(c, ALL), engs(c, ALL), "hfin")
    return nc


def build_hh1(D=2048, T=2048, TH=1024, H=32, DH=64, c=None, io=None):
    nd = D // 128
    nh = TH // 128
    nk = T // 128
    ng = cdiv(T, 512)
    sdh = 1.0 / math.sqrt(DH)
    c, es, io, standalone = _stage(c, io, [LN_EPS])
    nc = c.nc
    g, sy, sc, v, t = c.g, c.sy, c.sc, c.v, c.t
    if standalone:
        io["x"] = nc.dram_tensor("x", [D, T], BF16, kind="ExternalInput").ap()
        io["wqkv"] = nc.dram_tensor("wqkv", [34 * 128, nd * 128], F32, kind="ExternalInput").ap()
        io["colv"] = nc.dram_tensor("colv", [1, T], F32, kind="ExternalInput").ap()
        io["pssum"] = nc.dram_tensor("pssum", [TH, T], BF16, kind="ExternalOutput").ap()
        io["q_s"] = nc.dram_tensor("q_s", [32 * 128, TH], BF16, kind="ExternalOutput").ap()
        io["k_s"] = nc.dram_tensor("k_s", [128, T], BF16, kind="ExternalOutput").ap()
        io["v_s"] = nc.dram_tensor("v_s", [64, T], BF16, kind="ExternalOutput").ap()
    x, wqkv, colv_h = io["x"], io["wqkv"], io["colv"]
    pssum, q_s, k_s, v_s = io["pssum"], io["q_s"], io["k_s"], io["v_s"]

    with es:
        qall = c.sb("qall", [128, H * TH], BF16)
        kt = c.sb("kt", [128, T], BF16)
        cosb = c.sb("cosb", [128, (T // 128) * 32], BF16)
        sinb = c.sb("sinb", [128, (T // 128) * 32], BF16)
        with contextlib.ExitStack() as les_c:
            colv = c.sb("colv_s", [1, T], F32, les_c)
            s_c = c.sem("cl", "h")
            s_c.dinc(sy.dma_start(colv[:, :], colv_h[:, :]))
            s_c.wait(v)
            s_c.wait(t)

            def _freq_hh(jrow, v_, sc_, c_):
                v_.tensor_scalar(jrow[:, :].bitcast(I32), jrow[:, :].bitcast(I32),
                                 31, None, OP.bitwise_and)
                v_.drain()
                v_.tensor_copy(jrow[:, :], jrow[:, :].bitcast(I32))
                v_.drain()
                v_.tensor_scalar(jrow[:, :], jrow[:, :],
                                 -2.0 * math.log(10000.0) / 64.0, None, OP.mult)
                v_.drain()
            emit_rope_tables_tm(c, es, colv[:, :], T, 32, _freq_hh,
                                cosb, sinb, name="rh")
        with contextlib.ExitStack() as les_u:
            u = c.sb("u", [128, nd * T], BF16, les_u)
            with contextlib.ExitStack() as les_l:
                m_b, r_b = emit_ln_stats(c, les_l, x, D, T)
                emit_u(c, x, u, D, 0, T, m_b, r_b)
            emit_proj(c, u, T, wqkv[0:32 * 128, :], nd, 32, qall, TH, name="q")
            with contextlib.ExitStack() as les_kv:
                kv = c.sb("kv", [128, T], BF16, les_kv)
                emit_proj(c, u, T, wqkv[32 * 128:33 * 128, :], nd, 1, kt, T,
                          name="k")
                emit_proj(c, u, T, wqkv[33 * 128:34 * 128, :], nd, 1, kv, T,
                          name="v")
                s_vs = c.sem("vs", "g")
                s_vs.dinc(g.dma_start(v_s[:, :], kv[0:64, :]))
                g.wait_ge(s_vs.h, s_vs.n)
        identR = emit_identity(c, es, name="identR")
        tiles = []
        for h in range(H):
            for tc in range(TH // 128):
                tiles.append((qall[:, h * TH + tc * 128:h * TH + tc * 128 + 128], tc))
        for tc in range(T // 128):
            tiles.append((kt[:, tc * 128:(tc + 1) * 128], tc))
        emit_rope_fm(c, tiles, cosb, sinb, 32, identR, name="rph")
        s_qs = c.sem("qs", "g")
        for h in range(H):
            s_qs.dinc(g.dma_start(q_s[h * 128:(h + 1) * 128, :],
                                  qall[:, h * TH:(h + 1) * TH]))
        s_qs.dinc(g.dma_start(k_s[:, :], kt[:, :]))
        g.wait_ge(s_qs.h, s_qs.n)

        with contextlib.ExitStack() as les_p:
            ps_s = c.ps("ps_s", [128, T], F32, les_p)
            ps_a = c.ps("ps_a", [128, T], F32, les_p)
            prob = c.sb("prob", [128, T], BF16, les_p)
            acc_sb = c.sb("acc_sb", [128, 2 * T], BF16, les_p)
            identB = emit_identity(c, les_p)
            rmax = c.sb("rmax", [128, 1], F32, les_p)
            rsum = c.sb("rsum", [128, 1], F32, les_p)
            s_sc = c.sem("hsc")
            s_red = c.sem("hred")
            s_pb = c.sem("hpb")
            s_am = c.sem("ham")
            s_ae = c.sem("hae")
            s_st = [c.sem("hst0", "g"), c.sem("hst1", "g")]
            for qt in range(nh):
                for h in range(H):
                    qh = qall[:, h * TH + qt * 128:h * TH + qt * 128 + 128]
                    t.wait_ge(s_pb.h, s_pb.n)
                    for kg in range(ng):
                        t0, t1 = kg * 512, min(T, (kg + 1) * 512)
                        mm = t.matmul(ps_s[:, t0:t1], qh, kt[:, t0:t1],
                                      start=True, stop=True)
                    s_sc.inc(mm)
                    s_sc.wait(v)
                    v.tensor_reduce(rmax[:, :], ps_s[:, :], AX.X, OP.max)
                    v.drain()
                    s_red.inc(v.tensor_scalar(rmax[:, :], rmax[:, :], -sdh,
                                              None, OP.mult))
                    s_red.wait(sc)
                    s_pb.inc(sc.activation(prob[:, :], ps_s[:, :], AF.Exp,
                                           bias=rmax[:, :], scale=sdh,
                                           accum_out=rsum[:, :]))
                    s_pb.wait(v)
                    v.reciprocal(rsum[:, :], rsum[:, :])
                    v.drain()
                    s_pb.inc(v.tensor_scalar(prob[:, :], prob[:, :],
                                             rsum[:, :], None, OP.mult))
                    s_pb.wait(t)
                    if h == 0 and qt >= 1:
                        s_ae.wait(t)
                    for kg in range(ng):
                        t0, t1 = kg * 512, min(T, (kg + 1) * 512)
                        mm = t.matmul(ps_a[:, t0:t1], identB[:, :],
                                      prob[:, t0:t1],
                                      start=(h == 0), stop=(h == H - 1))
                    s_am.inc(mm)
                sl = qt % 2
                s_am.wait(sc)
                if qt >= 2:
                    s_st[sl].wait(sc)
                s_ae.inc(sc.copy(acc_sb[:, sl * T:sl * T + T], ps_a[:, :]))
                s_ae.wait(g)
                s_st[sl].dinc(g.dma_start(pssum[qt * 128:(qt + 1) * 128, :],
                                          acc_sb[:, sl * T:sl * T + T]))
            g.wait_ge(s_st[0].h, s_st[0].n)
            g.wait_ge(s_st[1].h, s_st[1].n)
            c.barrier(engs(c, ALL), engs(c, ALL), "h1fin")
    return nc


def build_hh3(D=2048, T=2048, TH=1024, H=32, DH=64, c=None, io=None):
    nd = D // 128
    nh = TH // 128
    nk = T // 128
    ng = cdiv(T, 512)
    sdh = 1.0 / math.sqrt(DH)
    c, es, io, standalone = _stage(c, io, [])
    nc = c.nc
    g, sy, sc, v, t = c.g, c.sy, c.sc, c.v, c.t
    if standalone:
        io["q_s"] = nc.dram_tensor("q_s", [32 * 128, TH], BF16, kind="ExternalInput").ap()
        io["k_s"] = nc.dram_tensor("k_s", [128, T], BF16, kind="ExternalInput").ap()
        io["v_s"] = nc.dram_tensor("v_s", [64, T], BF16, kind="ExternalInput").ap()
        io["maskT_my"] = nc.dram_tensor("maskT_my", [128, nk * TH], BF16, kind="ExternalInput").ap()
        io["wd"] = nc.dram_tensor("wd", [nd * 128, 32 * 128], F32, kind="ExternalInput").ap()
        io["rowv"] = nc.dram_tensor("rowv", [128, nh], F32, kind="ExternalInput").ap()
        io["colv"] = nc.dram_tensor("colv", [1, T], F32, kind="ExternalInput").ap()
        io["xr"] = nc.dram_tensor("xr", [D, TH], BF16, kind="ExternalInput").ap()
        io["xout"] = nc.dram_tensor("xout", [D, TH], BF16, kind="ExternalOutput").ap()
    q_h, k_h, v_h, mT_h = io["q_s"], io["k_s"], io["v_s"], io["maskT_my"]
    wd, rowv_h, colv_h, xr, xout = io["wd"], io["rowv"], io["colv"], io["xr"], io["xout"]

    with es:
        att = c.sb("att", [128, H * TH], BF16)
        extm = c.sb("extm", [128, nh * T], BF16)
        colb = c.sb("colb", [128, T], F32)
        rowv = c.sb("rowv_s", [128, nh], F32)
        rwm = c.sb("rwm", [128, nh], F32)
        ident = emit_identity(c, es)
        v.memset(att[:, :], 0.0)
        c.barrier([v], engs(c, ALL), "h3z")
        s_b = c.sem("meta", "h")
        s_b.dinc(sy.dma_start(rowv[:, :], rowv_h[:, :]))
        s_b.wait(v)
        v.tensor_scalar(rwm[:, :], rowv[:, :], -128.0, None, OP.add)
        c.barrier([v], engs(c, ALL), "h3b0")
        with contextlib.ExitStack() as les_c:
            colv = c.sb("colv_s", [1, T], F32, les_c)
            s_c = c.sem("cl", "h")
            s_c.dinc(sy.dma_start(colv[:, :], colv_h[:, :]))
            s_c.wait(v)
            s_c.wait(t)
            emit_bcast_row(c, colv[:, :], colb, T, "cb")
        with contextlib.ExitStack() as les_m:
            mTt = c.sb("mTt", [128, nk * TH], BF16, les_m)
            ps_t = c.ps("ps_t", [128, 2048], BF16, les_m)
            s_ld = c.sem("mld", "h")
            s_tp = c.sem("mtp")
            s_tc = c.sem("mtc")
            s_ld.dinc(sy.dma_start(mTt[:, :], mT_h[:, :]))
            s_ld.wait(t)
            for qt in range(nh):
                for ch2 in range(nk):
                    idx = qt * nk + ch2
                    off = (idx % 2) * 1024
                    if idx >= 2:
                        s_tc.wait_at(t, idx - 1)
                    s_tp.inc(t.transpose(ps_t[:, off:off + 128],
                                         mTt[:, ch2 * TH + qt * 128:ch2 * TH + qt * 128 + 128],
                                         ident[:, :]))
                    s_tp.wait(v)
                    s_tc.inc(v.tensor_copy(
                        extm[:, qt * T + ch2 * 128:qt * T + ch2 * 128 + 128],
                        ps_t[:, off:off + 128]))
            c.barrier([v], [v], "h3bm")
            with contextlib.ExitStack() as les_s:
                sc1 = c.sb("sc1", [128, T], BF16, les_s)
                for qt in range(nh):
                    em = extm[:, qt * T:(qt + 1) * T]
                    v.tensor_scalar(sc1[:, :], colb[:, :], rwm[:, qt:qt + 1],
                                    None, OP.is_ge)
                    v.drain()
                    v.tensor_tensor(em, em, sc1[:, :], OP.max)
                    v.drain()
                    v.tensor_scalar(sc1[:, :], colb[:, :], rowv[:, qt:qt + 1],
                                    None, OP.is_le)
                    v.drain()
                    v.tensor_tensor(em, em, sc1[:, :], OP.mult)
                    v.drain()
                    v.tensor_scalar(em, em, 1.0, 1e30, OP.subtract, OP.mult)
                    v.drain()
                c.barrier([v], engs(c, ALL), "h3bm2")

        with contextlib.ExitStack() as les_a:
            qhrot = c.sb("qhrot", [128, 2 * TH], BF16, les_a)
            kt = c.sb("kt", [128, T], BF16, les_a)
            vtr = c.sb("vtr", [128, nk * 64], BF16, les_a)
            s_ld2 = [c.sem("qld0", "h"), c.sem("qld1", "h")]
            s_kld = c.sem("kld", "h")
            s_kld.dinc(sy.dma_start(kt[:, :], k_h[:, :]))
            s_kld.wait(t)
            with contextlib.ExitStack() as les_v:
                vt_ = c.sb("vt_", [64, T], BF16, les_v)
                s_lv = c.sem("vld", "h")
                s_lv.dinc(sy.dma_start(vt_[:, :], v_h[:, :]))
                s_lv.wait(t)
                emit_transposes(
                    c, [(vt_[:, kc * 128:(kc + 1) * 128],
                         vtr[:, kc * 64:(kc + 1) * 64]) for kc in range(nk)],
                    ident, "vt", kdim=64)
            ps_s = c.ps("ps_s", [128, T], F32, les_a)
            ps_t2 = c.ps("ps_t2", [128, 2048], BF16, les_a)
            ps_o = c.ps("ps_o", [64, 512], F32, les_a)
            prob = c.sb("prob", [128, T], BF16, les_a)
            prT = c.sb("prT", [128, 4 * T], BF16, les_a)
            rmax = c.sb("rmax", [128, 1], F32, les_a)
            rsum = c.sb("rsum", [128, 1], F32, les_a)
            QGW = min(512, TH)
            nqi = QGW // 128
            s_sc = c.sem("asc")
            s_red = c.sem("ared")
            s_pb = c.sem("apb")
            s_tp2 = c.sem("atp")
            s_tc2 = c.sem("atc")
            s_av = c.sem("aav")
            s_ao = c.sem("aao")
            for h in range(H):
                sl = h % 2
                qh_t = qhrot[:, sl * TH:sl * TH + TH]
                if h >= 2:
                    s_av.wait_at(sy, (h - 1) * (TH // QGW))
                s_ld2[sl].dinc(sy.dma_start(qh_t, q_h[h * 128:(h + 1) * 128, :]))
                s_ld2[sl].wait(t)
                for qg in range(TH // QGW):
                    for qi in range(nqi):
                        qt = qg * nqi + qi
                        qh = qh_t[:, qt * 128:qt * 128 + 128]
                        t.wait_ge(s_pb.h, s_pb.n)
                        for kg in range(ng):
                            t0, t1 = kg * 512, min(T, (kg + 1) * 512)
                            mm = t.matmul(ps_s[:, t0:t1], qh, kt[:, t0:t1],
                                          start=True, stop=True)
                        s_sc.inc(mm)
                        s_sc.wait(v)
                        v.tensor_tensor(ps_s[:, :], ps_s[:, :],
                                        extm[:, qt * T:(qt + 1) * T], OP.add)
                        v.drain()
                        v.tensor_reduce(rmax[:, :], ps_s[:, :], AX.X, OP.max)
                        v.drain()
                        s_red.inc(v.tensor_scalar(rmax[:, :], rmax[:, :], -sdh,
                                                  None, OP.mult))
                        s_red.wait(sc)
                        sc.wait_ge(s_tp2.h, s_tp2.n)
                        s_pb.inc(sc.activation(prob[:, :], ps_s[:, :], AF.Exp,
                                               bias=rmax[:, :], scale=sdh,
                                               accum_out=rsum[:, :]))
                        s_pb.wait(v)
                        v.reciprocal(rsum[:, :], rsum[:, :])
                        v.drain()
                        s_pb.inc(v.tensor_scalar(prob[:, :], prob[:, :],
                                                 rsum[:, :], None, OP.mult))
                        s_pb.wait(t)
                        prTv = prT[:].rearrange("p (kc q) -> p kc q", kc=nk)
                        ntb = s_tc2.n
                        for kb in range(nk // 8):
                            off = (kb % 2) * 1024
                            t.wait_ge(s_tc2.h, max(0, ntb - 1 + kb))
                            for j in range(8):
                                kc = kb * 8 + j
                                mm = t.transpose(
                                    ps_t2[:, off + j * 128:off + (j + 1) * 128],
                                    prob[:, kc * 128:(kc + 1) * 128], ident[:, :])
                            s_tp2.inc(mm)
                            v.wait_ge(s_tp2.h, s_tp2.n)
                            s_tc2.inc(v.tensor_copy(
                                prTv[:, kb * 8:(kb + 1) * 8, qi * 128:qi * 128 + 128],
                                ps_t2[:, off:off + 1024].rearrange(
                                    "p (a q) -> p a q", a=8)))
                    s_tc2.wait(t)
                    t.wait_ge(s_ao.h, s_ao.n)
                    for kc in range(nk):
                        mm = t.matmul(ps_o[:, 0:QGW],
                                      vtr[:, kc * 64:(kc + 1) * 64],
                                      prT[:, kc * QGW:(kc + 1) * QGW],
                                      start=(kc == 0), stop=(kc == nk - 1))
                    s_av.inc(mm)
                    s_av.wait(v)
                    s_ao.inc(v.tensor_copy(
                        att[0:64, h * TH + qg * QGW:h * TH + qg * QGW + QGW],
                        ps_o[:, 0:QGW]))
                    t.wait_ge(s_ao.h, s_ao.n)
            c.barrier([v], engs(c, ALL), "h3ba")
        ysb = c.sb("ysb", [128, nd * TH], BF16)
        emit_proj(c, att, TH, wd, H, nd, ysb, TH, name="o")
        emit_residual_out(c, xr, ysb, nd, TH, xout, xr_dt=BF16)
    return nc



def prep_w(w_t):
    Din, Dout = w_t.shape
    ndd, nout = Din // 128, Dout // 128
    return np.ascontiguousarray(
        w_t.reshape(ndd, 128, nout, 128).transpose(2, 1, 0, 3).reshape(nout * 128, ndd * 128))


def prep_bias(b):
    return np.ascontiguousarray(np.asarray(b).reshape(-1, 128).T)


def prep_fp4_rep(packed, nout, nin):
    out = np.empty((nout, 128, nin * 128), np.int32)
    for j in range(nout):
        blk = packed[j * 128:(j + 1) * 128].T
        out[j] = np.repeat(blk.reshape(nin, 16, 128).transpose(1, 0, 2),
                           8, axis=0).reshape(128, nin * 128)
    return out


def prep_fp4_scl(scales, nout, nin):
    out = np.empty((nout, nin * 128), np.float16)
    for j in range(nout):
        out[j] = scales[:, j * 128:(j + 1) * 128].astype(np.float16).ravel()
    return out


def eo_perm(D=2048, DH=128):
    idx = []
    for h in range(D // DH):
        base = h * DH
        idx.extend(range(base, base + DH, 2))
        idx.extend(range(base + 1, base + DH, 2))
    return np.array(idx)


def prep_wqkv_hh(w_qkv):
    D = 2048
    wpad = np.zeros((34 * 128, D), np.float32)
    for h in range(32):
        wpad[h * 128:h * 128 + 64] = w_qkv[h * 64:(h + 1) * 64]
    wpad[32 * 128:32 * 128 + 64] = w_qkv[2048:2112]
    wpad[33 * 128:33 * 128 + 64] = w_qkv[2112:2176]
    return prep_w(np.ascontiguousarray(wpad.T))



B, SEQ, DM, THALF = 4, 2048, 2048, 1024


def _run(nc, in_maps):
    return run_bass_kernel_spmd(nc, in_maps, core_ids=list(range(8))).results


def _halves(cid):
    b, h = cid // 2, cid % 2
    my = np.arange(h * THALF, (h + 1) * THALF)
    oth = np.arange((1 - h) * THALF, (2 - h) * THALF)
    return b, h, my, np.concatenate([my, oth])


def build_full(debug=False, ncores=8, marks=None):
    D, T, TH = DM, SEQ, THALF
    nd = D // 128

    def mark(name):
        if marks is not None:
            marks.append((name, sum(1 for _ in nc.all_instructions())))

    GRP8 = [list(range(ncores))]
    PRS = [[2 * i, 2 * i + 1] for i in range(ncores // 2)]
    nchg, nchd = 88 // ncores, 16 // ncores
    nc = bass.Bass("TRN2", target_bir_lowering=False)
    es = contextlib.ExitStack()
    c = Ctx(nc, es)

    def ein(n, s, dt):
        return nc.dram_tensor(n, s, dt, kind="ExternalInput").ap()

    def mid(n, s, dt):
        return nc.dram_tensor(n, s, dt).ap()

    with es:
        c.init_consts([LN_EPS, 1.0, 1e-6])
        selh = ein("selh", [128, 2], F32)
        selt = c.sb("selt", [128, 2], F32)
        s0 = c.sem("sel0", "h")
        s0.dinc(c.sy.dma_start(selt[:, :], selh[:, :]))
        s0.wait(c.v)
        sel, selbar = selt[:, 0:1], selt[:, 1:2]

        xs = {k: mid(f"xs{k}", [D, TH], BF16) for k in range(1, 8)}
        xperm2 = mid("xperm2", [D, T], BF16)
        xperm6 = mid("xperm6", [D, T], BF16)
        xg2 = mid("xg2", [2 * D, TH], BF16)
        xg4 = mid("xg4", [2 * D, TH], BF16)
        xg6 = mid("xg6", [2 * D, TH], BF16)
        xext = mid("xext", [D, T + 3], BF16)
        gu_loc = mid("gu_loc", [nchg * 128, 2048], BF16)
        dn_loc = mid("dn_loc", [nchd * 128, 5632], BF16)
        wgu_full = mid("wgu_full", [88 * 128, 2048], BF16)
        wdn_full = mid("wdn_full", [16 * 128, 5632], BF16)
        dbc_loc = mid("dbc_loc", [160, T], F32)
        dbc_full = mid("dbc_full", [160, T], F32)
        xc_s = mid("xc_s", [2048, T], BF16)
        dl_s = mid("dl_s", [2048, T], BF16)
        z_s = mid("z_s", [2048, T], BF16)
        bc_s = mid("bc_s", [32, T], F32)
        y_s = mid("y_s", [2048, T], BF16)
        yg = mid("yg", [2 * 2048, T], BF16)
        y_m3 = mid("y_m3", [4096, TH], BF16)
        pssum = mid("pssum", [TH, T], BF16)
        psg = mid("psg", [2 * TH, T], BF16)
        ps1m = mid("ps1m", [TH, T], BF16)
        q_s = mid("q_s", [32 * 128, TH], BF16)
        k_s = mid("k_s", [128, T], BF16)
        v_s = mid("v_s", [64, T], BF16)
        k_nat = mid("k_nat", [128, T], BF16)
        v_nat = mid("v_nat", [64, T], BF16)
        maskT = mid("maskT", [128, 16 * T], BF16)
        mT_my = mid("mT_my", [128, 16 * TH], BF16)
        xout_f = nc.dram_tensor("xout", [D, TH], F32, kind="ExternalOutput").ap()

        aa_io = {"x": ein("aa_x", [D, T], F32),
                 "wq": ein("aa_wq", [2048, D], F32),
                 "wk": ein("aa_wk", [512, D], F32),
                 "wv": ein("aa_wv", [512, D], F32),
                 "bq": ein("aa_bq", [128, 16], F32),
                 "bk": ein("aa_bk", [128, 4], F32),
                 "bv": ein("aa_bv", [128, 4], F32),
                 "rowv": ein("rowv", [128, TH // 128], F32),
                 "colv": ein("colv", [1, T], F32),
                 "xr": ein("aa_xr", [D, TH], F32),
                 "xout": xs[1]}
        dq_io = {"rgu": ein("rgu", [nchg * 128, 2048], I32),
                 "sgu": ein("sgu", [nchg, 2048], mybir.dt.float16),
                 "rdn": ein("rdn", [nchd * 128, 5632], I32),
                 "sdn": ein("sdn", [nchd, 5632], mybir.dt.float16),
                 "dqgu": gu_loc, "dqdn": dn_loc}
        ab_io = {"x": xperm2,
                 "wq": ein("ab_wq", [D, D], F32),
                 "wk": ein("ab_wk", [D, D], F32),
                 "wv": ein("ab_wv", [D, D], F32),
                 "wo": ein("ab_wo", [D, D], F32),
                 "rowv": aa_io["rowv"], "colv": aa_io["colv"],
                 "xr": xs[2], "xout": xs[3]}
        m1_io = {"x": xext,
                 "win": ein("m1_win", [2 * 2048, D], F32),
                 "cw": ein("m1_cw", [128, 16 * 4], F32),
                 "cb": ein("m1_cb", [128, 16], F32),
                 "wxd": ein("m1_wxd", [128, 16 * 128], F32),
                 "wxbc": ein("m1_wxbc", [128, 16 * 32], F32),
                 "wdt": ein("m1_wdt", [2048, 128], F32),
                 "bdt": ein("m1_bdt", [128, 16], F32),
                 "xc_s": xc_s, "dl_s": dl_s, "z_s": z_s, "bc_s": bc_s}
        m2_io = {"xc": xc_s, "dl": dl_s, "z": z_s, "bc": bc_s,
                 "a_t": ein("a_t", [128, 256], F32),
                 "dp": ein("dp", [128, 16], F32),
                 "y_s": y_s}
        m3_io = {"y": y_m3, "wout": ein("wout", [D, 32 * 128], F32),
                 "x": xs[4], "xout": xs[5]}
        moe_io = {"wdn": ein("moe_wdn", [1024, D], F32),
                  "bdn": ein("moe_bdn", [128, 8], F32),
                  "wrt": ein("moe_wrt", [128, nd * 8], F32),
                  "brt": ein("moe_brt", [8, 1], F32),
                  "wup": ein("moe_wup", [D, 1024], F32),
                  "bup": ein("moe_bup", [128, nd], F32),
                  "wcr": ein("moe_wcr", [D, D], F32),
                  "bcr": ein("moe_bcr", [128, nd], F32)}
        for e, he in enumerate(MOE_HID + [1024]):
            moe_io[f"w1_{e}"] = ein(f"w1_{e}", [he, 1024], F32)
            moe_io[f"b1_{e}"] = ein(f"b1_{e}", [128, he // 128], F32)
            moe_io[f"w2_{e}"] = ein(f"w2_{e}", [1024, he], F32)
            moe_io[f"b2_{e}"] = ein(f"b2_{e}", [128, 8], F32)
        hh1_io = {"x": xperm6,
                  "wqkv": ein("wqkv", [34 * 128, D], F32),
                  "colv": aa_io["colv"],
                  "pssum": pssum, "q_s": q_s, "k_s": k_s, "v_s": v_s}
        hh2_io = {"ps0": psg[0:TH, :], "ps1": ps1m, "maskT": maskT}
        hh3_io = {"q_s": q_s, "k_s": k_nat, "v_s": v_nat,
                  "maskT_my": mT_my,
                  "wd": ein("wd", [nd * 128, 32 * 128], F32),
                  "rowv": aa_io["rowv"],
                  "colv": ein("hh3_colv", [1, T], F32),
                  "xr": xs[6], "xout": xs[7]}

        build_dequant(nch_gu=nchg, nch_dn=nchd, c=c, io=dq_io)
        mark("dequant")
        s_ag_gu = emit_cc(c, "AllGather", [gu_loc], [wgu_full], GRP8)
        s_ag_dn = emit_cc(c, "AllGather", [dn_loc], [wdn_full], GRP8)
        build_attn_a(c=c, io=aa_io)
        mark("attn_a")
        emit_cc_wait(c, s_ag_gu)
        emit_cc_wait(c, s_ag_dn)
        build_ffn(c=c, io={"x": xs[1], "wgu": wgu_full, "wdn": wdn_full,
                           "xout": xs[2]})
        mark("ffn1")
        s_x2 = emit_cc(c, "AllGather", [xs[2]], [xg2], PRS)
        emit_cc_wait(c, s_x2)
        emit_copy(c, [(xperm2[:, 0:TH], xs[2][:, :])], name="xp2a")
        emit_sel_combine(
            c, [(xg2[i * 128:(i + 1) * 128, :],
                 xg2[D + i * 128:D + i * 128 + 128, :],
                 xperm2[i * 128:(i + 1) * 128, TH:T]) for i in range(nd)],
            TH, sel, selbar, name="xp2s")
        build_attn_b(c=c, io=ab_io)
        mark("attn_b")
        build_ffn(c=c, io={"x": xs[3], "wgu": wgu_full, "wdn": wdn_full,
                           "xout": xs[4]})
        mark("ffn2")

        s_x4 = emit_cc(c, "AllGather", [xs[4]], [xg4], PRS)
        emit_cc_wait(c, s_x4)
        with contextlib.ExitStack() as lz:
            zt_ = c.sb("zt0", [128, 3], BF16, lz)
            c.v.memset(zt_[:, :], 0.0)
            c.barrier([c.v], [c.sy], "zb")
            emit_copy(c, [(xext[i * 128:(i + 1) * 128, 0:3], zt_[:, :])
                          for i in range(nd)]
                      + [(xext[:, 3:3 + TH], xg4[0:D, :]),
                         (xext[:, 3 + TH:3 + T], xg4[D:2 * D, :])],
                      name="xexb")
        build_mamba1(D=D, TH=T, E=2048, c=c, io=m1_io, pad_zero=True,
                     ar={"loc": dbc_loc, "full": dbc_full, "groups": PRS})
        mark("mamba1")
        build_mamba2(EH=2048, T=T, c=c, io=m2_io)
        mark("mamba2")
        s_y = emit_cc(c, "AllGather", [y_s], [yg], PRS)
        emit_cc_wait(c, s_y)
        emit_sel_combine(
            c, [(yg[i * 128:(i + 1) * 128, 0:TH],
                 yg[i * 128:(i + 1) * 128, TH:T],
                 y_m3[i * 128:(i + 1) * 128, :]) for i in range(32)],
            TH, selbar, sel, name="ysel")
        build_mamba3(c=c, io=m3_io)
        mark("mamba3")

        build_moe(out_dt=BF16, c=c, io={**moe_io, "x": xs[5], "xout": xs[6]})
        mark("moe1")

        s_x6 = emit_cc(c, "AllGather", [xs[6]], [xg6], PRS)
        emit_cc_wait(c, s_x6)
        emit_copy(c, [(xperm6[:, 0:TH], xs[6][:, :])], name="xp6a")
        emit_sel_combine(
            c, [(xg6[i * 128:(i + 1) * 128, :],
                 xg6[D + i * 128:D + i * 128 + 128, :],
                 xperm6[i * 128:(i + 1) * 128, TH:T]) for i in range(nd)],
            TH, sel, selbar, name="xp6s")
        build_hh1(c=c, io=hh1_io)
        mark("hh1")
        s_ps = emit_cc(c, "AllGather", [pssum], [psg], PRS)
        emit_cc_wait(c, s_ps)
        emit_copy(c, [(ps1m[:, 0:TH], psg[TH:T, TH:T]),
                      (ps1m[:, TH:T], psg[TH:T, 0:TH])], name="ps1c")
        build_hh2(c=c, io=hh2_io)
        mark("hh2")
        emit_sel_combine(
            c, [(maskT[:, j * T:j * T + TH],
                 maskT[:, j * T + TH:(j + 1) * T],
                 mT_my[:, j * TH:(j + 1) * TH]) for j in range(16)],
            TH, selbar, sel, name="mTs")
        emit_sel_combine(
            c, [(k_s[:, 0:TH], k_s[:, TH:T], k_nat[:, 0:TH]),
                (k_s[:, TH:T], k_s[:, 0:TH], k_nat[:, TH:T])],
            TH, selbar, sel, name="kns")
        emit_sel_combine(
            c, [(v_s[:, 0:TH], v_s[:, TH:T], v_nat[:, 0:TH]),
                (v_s[:, TH:T], v_s[:, 0:TH], v_nat[:, TH:T])],
            TH, selbar, sel, name="vns", P=64)
        build_hh3(c=c, io=hh3_io)
        mark("hh3")

        build_moe(out_dt=F32, c=c, io={**moe_io, "x": xs[7], "xout": xout_f})
        mark("moe2")

        if debug:
            dbg = {k: nc.dram_tensor(f"dbg{k}", [D, TH], BF16,
                                     kind="ExternalOutput").ap()
                   for k in range(1, 8)}
            emit_copy(c, [(dbg[k], xs[k][:, :]) for k in range(1, 8)],
                      name="dbgc")
    return nc


def _host_prep(ii):
    Wq0 = ii["L0_attn_a_Wq"]; Wk0 = ii["L0_attn_a_Wk"]; Wv0 = ii["L0_attn_a_Wv"]
    aa = {
        "wq": prep_w(Wq0.transpose(1, 0, 2).reshape(DM, -1)),
        "wk": prep_w(Wk0.transpose(1, 0, 2).reshape(DM, -1)),
        "wv": prep_w(Wv0.transpose(1, 0, 2).reshape(DM, -1)),
        "bq": prep_bias(ii["L0_attn_a_bq"].ravel()),
        "bk": prep_bias(ii["L0_attn_a_bk"].ravel()),
        "bv": prep_bias(ii["L0_attn_a_bv"].ravel()),
    }
    pe = eo_perm()
    ab = {
        "wq": prep_w(np.ascontiguousarray(ii["L0_attn_b_Wq"].T)[:, pe]),
        "wk": prep_w(np.ascontiguousarray(ii["L0_attn_b_Wk"].T)[:, pe]),
        "wv": prep_w(np.ascontiguousarray(ii["L0_attn_b_Wv"].T)),
        "wo": prep_w(np.ascontiguousarray(ii["L0_attn_b_Wo"].T)),
    }
    dq_in = {
        "rgu": prep_fp4_rep(ii["L0_ffn_gate_up_packed"], 88, 16),
        "sgu": prep_fp4_scl(np.asarray(ii["L0_ffn_gate_up_scales"], np.float16), 88, 16),
        "rdn": prep_fp4_rep(ii["L0_ffn_down_packed"], 16, 44),
        "sdn": prep_fp4_scl(np.asarray(ii["L0_ffn_down_scales"], np.float16), 16, 44),
    }
    m1c = {
        "win": prep_w(np.ascontiguousarray(ii["L1_attn_a_W_in"].T)),
        "cw": np.ascontiguousarray(ii["L1_attn_a_conv_w"].reshape(32, 128, 4)
                                   .transpose(1, 0, 2).reshape(128, 128)),
        "cb": prep_bias(ii["L1_attn_a_conv_b"]),
        "wxd": np.ascontiguousarray(ii["L1_attn_a_W_x"][0:128].T
                                    .reshape(32, 128, 128).transpose(1, 0, 2).reshape(128, 32 * 128)),
        "wxbc": np.ascontiguousarray(ii["L1_attn_a_W_x"][128:160].T
                                     .reshape(32, 128, 32).transpose(1, 0, 2).reshape(128, 32 * 32)),
        "wdt": prep_w(np.ascontiguousarray(ii["L1_attn_a_W_dt"].T)),
        "bdt": prep_bias(ii["L1_attn_a_b_dt"]),
    }
    wout_p = prep_w(np.ascontiguousarray(ii["L1_attn_a_W_out"].T))
    A_log, Dp = ii["L1_attn_a_A_log"], ii["L1_attn_a_Dp"]
    moe_in = {
        "wdn": prep_w(ii["L1_ffn_down_w"]),
        "bdn": prep_bias(ii["L1_ffn_down_b"]),
        "wrt": np.ascontiguousarray(ii["L1_ffn_router_w"].reshape(16, 128, 8)
                                    .transpose(1, 0, 2).reshape(128, 128)),
        "brt": np.ascontiguousarray(ii["L1_ffn_router_b"].reshape(8, 1)),
        "wup": prep_w(ii["L1_ffn_up_w"]),
        "bup": prep_bias(ii["L1_ffn_up_b"]),
        "wcr": prep_w(ii["L1_ffn_core_w"]),
        "bcr": prep_bias(ii["L1_ffn_core_b"]),
    }
    for e in range(9):
        if e < 8:
            w1, b1 = ii[f"L1_ffn_e{e}_w1"], ii[f"L1_ffn_e{e}_b1"]
            w2, b2 = ii[f"L1_ffn_e{e}_w2"], ii[f"L1_ffn_e{e}_b2"]
        else:
            w1, b1 = ii["L1_ffn_shared_w1"], ii["L1_ffn_shared_b1"]
            w2, b2 = ii["L1_ffn_shared_w2"], ii["L1_ffn_shared_b2"]
        moe_in[f"w1_{e}"] = prep_w(w1)
        moe_in[f"b1_{e}"] = prep_bias(b1)
        moe_in[f"w2_{e}"] = prep_w(w2)
        moe_in[f"b2_{e}"] = prep_bias(b2)
    wqkv_p = prep_wqkv_hh(ii["L1_attn_b_w_qkv"])
    wdT = np.ascontiguousarray(ii["L1_attn_b_w_dense"].T)
    wd_pad = np.zeros((32 * 128, DM), np.float32)
    for hh in range(32):
        wd_pad[hh * 128:hh * 128 + 64] = wdT[hh * 64:(hh + 1) * 64]
    wd_p = prep_w(wd_pad)
    return dict(aa=aa, ab=ab, dq_in=dq_in, m1c=m1c, wout_p=wout_p,
                A_log=A_log, Dp=Dp, moe_in=moe_in, wqkv_p=wqkv_p, wd_p=wd_p)


def _m2_at_dp(A_log, Dp, h, EH=2048):
    sl = slice(h * EH, (h + 1) * EH)
    a_t = np.empty((128, 256), np.float32)
    p = np.arange(128)
    for ch in range(16):
        for j in range(16):
            a_t[:, ch * 16 + j] = A_log[h * EH + 128 * ch + 8 * j + p // 16, p % 16]
    return a_t, prep_bias(Dp[sl])


def rowv_of(my):
    return np.ascontiguousarray(my.astype(np.float32).reshape(-1, 128).T)


def _fused_inmaps(ii, ncores=8):
    pp = _host_prep(ii)
    aa, ab, dq_in, moe_in = pp["aa"], pp["ab"], pp["dq_in"], pp["moe_in"]
    m1c, A_log, Dp = pp["m1c"], pp["A_log"], pp["Dp"]

    pw_in = m1c["win"]
    W_x = ii["L1_attn_a_W_x"]
    conv_w, conv_b = ii["L1_attn_a_conv_w"], ii["L1_attn_a_conv_b"]
    b_dt = ii["L1_attn_a_b_dt"]
    wdt_p = m1c["wdt"]
    cb_p = m1c["cb"]
    bdt_p = m1c["bdt"]
    m1cs = []
    for h in range(2):
        r0 = h * 16 * 128
        z0 = (32 + h * 16) * 128
        sl = slice(h * 2048, (h + 1) * 2048)
        m1cs.append({
            "m1_win": np.ascontiguousarray(
                np.concatenate([pw_in[r0:r0 + 16 * 128], pw_in[z0:z0 + 16 * 128]])),
            "m1_cw": np.ascontiguousarray(
                conv_w[sl].reshape(16, 128, 4).transpose(1, 0, 2).reshape(128, 64)),
            "m1_cb": np.ascontiguousarray(cb_p[:, h * 16:(h + 1) * 16]),
            "m1_wxd": np.ascontiguousarray(
                W_x[0:128, sl].T.reshape(16, 128, 128).transpose(1, 0, 2)
                .reshape(128, 16 * 128)),
            "m1_wxbc": np.ascontiguousarray(
                W_x[128:160, sl].T.reshape(16, 128, 32).transpose(1, 0, 2)
                .reshape(128, 16 * 32)),
            "m1_wdt": np.ascontiguousarray(wdt_p[r0:r0 + 16 * 128]),
            "m1_bdt": np.ascontiguousarray(bdt_p[:, h * 16:(h + 1) * 16]),
        })
    atdp = [_m2_at_dp(A_log, Dp, h) for h in range(2)]

    x0 = ii["L0_attn_a_x"]
    xT0 = [np.ascontiguousarray(x0[b].T) for b in range(B)]
    nat_colv = np.arange(SEQ, dtype=np.float32)[None, :]

    nchg, nchd = 88 // ncores, 16 // ncores
    im = []
    for cid in range(ncores):
        b, h, my, perm = _halves(cid)
        d = {"selh": np.stack([np.full(128, h, np.float32),
                               np.full(128, 1.0 - h, np.float32)], 1)}
        d["aa_x"] = np.ascontiguousarray(xT0[b][:, perm])
        d["aa_xr"] = np.ascontiguousarray(xT0[b][:, my])
        for k in ("wq", "wk", "wv", "bq", "bk", "bv"):
            d[f"aa_{k}"] = aa[k]
        d["rowv"] = rowv_of(my)
        d["colv"] = perm.astype(np.float32)[None, :]
        d["rgu"] = dq_in["rgu"][nchg * cid:nchg * (cid + 1)].reshape(nchg * 128, 2048)
        d["sgu"] = dq_in["sgu"][nchg * cid:nchg * (cid + 1)]
        d["rdn"] = dq_in["rdn"][nchd * cid:nchd * (cid + 1)].reshape(nchd * 128, 5632)
        d["sdn"] = dq_in["sdn"][nchd * cid:nchd * (cid + 1)]
        for k in ("wq", "wk", "wv", "wo"):
            d[f"ab_{k}"] = ab[k]
        d.update(m1cs[h])
        d["a_t"], d["dp"] = atdp[h]
        d["wout"] = pp["wout_p"]
        for k, vv in moe_in.items():
            key = f"moe_{k}" if k in ("wdn", "bdn", "wrt", "brt", "wup",
                                      "bup", "wcr", "bcr") else k
            d[key] = vv
        d["wqkv"] = pp["wqkv_p"]
        d["wd"] = pp["wd_p"]
        d["hh3_colv"] = nat_colv
        im.append(d)
    return im


def kernel(**inputs):
    ii = {k: np.asarray(v) for k, v in inputs.items()}
    im = _fused_inmaps(ii)
    nc = build_full()
    outs = _run(nc, im)
    out = np.empty((B, SEQ, DM), np.float32)
    for cid in range(8):
        b, h, my, _ = _halves(cid)
        out[b, my] = np.asarray(outs[cid]["xout"]).T
    return out


def kernel_unfused(**inputs):
    ii = {k: np.asarray(v) for k, v in inputs.items()}
    pp = _host_prep(ii)
    aa, ab, dq_in, moe_in = pp["aa"], pp["ab"], pp["dq_in"], pp["moe_in"]
    m1c, wout_p, A_log, Dp = pp["m1c"], pp["wout_p"], pp["A_log"], pp["Dp"]
    wqkv_p, wd_p = pp["wqkv_p"], pp["wd_p"]

    x0 = ii["L0_attn_a_x"]
    xT0 = [np.ascontiguousarray(x0[b].T) for b in range(B)]

    def assemble(outs, key="xout"):
        xs = []
        for b in range(B):
            xs.append(np.concatenate(
                [np.asarray(outs[2 * b][key]).view(BF),
                 np.asarray(outs[2 * b + 1][key]).view(BF)], axis=1))
        return xs

    nc = build_attn_a()
    im = []
    for cid in range(8):
        b, h, my, perm = _halves(cid)
        im.append({**aa,
                   "x": np.ascontiguousarray(xT0[b][:, perm]),
                   "rowv": rowv_of(my),
                   "colv": perm.astype(np.float32)[None, :],
                   "xr": np.ascontiguousarray(xT0[b][:, my])})
    x_cur = assemble(_run(nc, im))

    nc = build_dequant()
    im = []
    for cid in range(8):
        im.append({"rgu": dq_in["rgu"][11 * cid:11 * (cid + 1)].reshape(11 * 128, 2048),
                   "sgu": dq_in["sgu"][11 * cid:11 * (cid + 1)],
                   "rdn": dq_in["rdn"][2 * cid:2 * (cid + 1)].reshape(2 * 128, 5632),
                   "sdn": dq_in["sdn"][2 * cid:2 * (cid + 1)]})
    dq = _run(nc, im)
    wgu_d = np.concatenate([np.asarray(dq[c]["dqgu"]).view(BF) for c in range(8)], axis=0)
    wdn_d = np.concatenate([np.asarray(dq[c]["dqdn"]).view(BF) for c in range(8)], axis=0)

    nc_ffn = build_ffn()

    def run_ffn(xs):
        im = []
        for cid in range(8):
            b, h, my, _ = _halves(cid)
            im.append({"x": np.ascontiguousarray(xs[b][:, my]),
                       "wgu": wgu_d, "wdn": wdn_d})
        return assemble(_run(nc_ffn, im))

    x_cur = run_ffn(x_cur)

    nc = build_attn_b()
    im = []
    for cid in range(8):
        b, h, my, perm = _halves(cid)
        im.append({**ab,
                   "x": np.ascontiguousarray(x_cur[b][:, perm]),
                   "rowv": rowv_of(my),
                   "colv": perm.astype(np.float32)[None, :],
                   "xr": np.ascontiguousarray(x_cur[b][:, my])})
    x_cur = assemble(_run(nc, im))

    x_cur = run_ffn(x_cur)

    nc = build_mamba1()
    im = []
    for cid in range(8):
        b, h, my, _ = _halves(cid)
        if h == 0:
            x_ext = np.concatenate([np.zeros((DM, 3), BF), x_cur[b][:, 0:THALF]], 1)
        else:
            x_ext = x_cur[b][:, THALF - 3:2 * THALF]
        im.append({**m1c, "x": np.ascontiguousarray(x_ext)})
    o1 = _run(nc, im)

    nc = build_mamba2()
    EH = 2048
    im = []
    for cid in range(8):
        b, h, _, _ = _halves(cid)
        sl = slice(h * EH, (h + 1) * EH)
        c0, c1 = o1[2 * b], o1[2 * b + 1]
        a_t = np.empty((128, 256), np.float32)
        p = np.arange(128)
        for ch in range(16):
            for j in range(16):
                a_t[:, ch * 16 + j] = A_log[h * EH + 128 * ch + 8 * j + p // 16, p % 16]
        im.append({
            "xc": np.ascontiguousarray(np.concatenate(
                [np.asarray(c0["xc_s"]).view(BF)[sl], np.asarray(c1["xc_s"]).view(BF)[sl]], 1)),
            "dl": np.ascontiguousarray(np.concatenate(
                [np.asarray(c0["dl_s"]).view(BF)[sl], np.asarray(c1["dl_s"]).view(BF)[sl]], 1)),
            "z": np.ascontiguousarray(np.concatenate(
                [np.asarray(c0["z_s"]).view(BF)[sl], np.asarray(c1["z_s"]).view(BF)[sl]], 1)),
            "bc": np.ascontiguousarray(np.concatenate(
                [np.asarray(c0["bc_s"]), np.asarray(c1["bc_s"])], 1)),
            "a_t": a_t, "dp": prep_bias(Dp[sl])})
    o2 = _run(nc, im)

    nc = build_mamba3()
    im = []
    for cid in range(8):
        b, h, my, _ = _halves(cid)
        y = np.concatenate([np.asarray(o2[2 * b]["y_s"]).view(BF)[:, h * THALF:(h + 1) * THALF],
                            np.asarray(o2[2 * b + 1]["y_s"]).view(BF)[:, h * THALF:(h + 1) * THALF]],
                           axis=0)
        im.append({"y": np.ascontiguousarray(y), "wout": wout_p,
                   "x": np.ascontiguousarray(x_cur[b][:, my])})
    x_cur = assemble(_run(nc, im))

    nc_moe = build_moe(out_dt=BF16)
    im = []
    for cid in range(8):
        b, h, my, _ = _halves(cid)
        im.append({**moe_in, "x": np.ascontiguousarray(x_cur[b][:, my])})
    x_cur = assemble(_run(nc_moe, im))

    nc = build_hh1()
    im = []
    for cid in range(8):
        b, h, my, perm = _halves(cid)
        im.append({"x": np.ascontiguousarray(x_cur[b][:, perm]), "wqkv": wqkv_p,
                   "colv": perm.astype(np.float32)[None, :]})
    oh1 = _run(nc, im)

    perm1 = np.concatenate([np.arange(THALF, SEQ), np.arange(0, THALF)])
    inv1 = np.empty(SEQ, np.int64)
    inv1[perm1] = np.arange(SEQ)

    nc = build_hh2()
    im = []
    for cid in range(8):
        b, h, _, _ = _halves(cid)
        ps0 = np.asarray(oh1[2 * b]["pssum"]).view(BF)
        ps1 = np.ascontiguousarray(np.asarray(oh1[2 * b + 1]["pssum"]).view(BF)[:, inv1])
        im.append({"ps0": ps0, "ps1": ps1})
    oh2 = _run(nc, im)

    nc = build_hh3()
    im = []
    for cid in range(8):
        b, h, my, _ = _halves(cid)
        maskT = np.asarray(oh2[2 * b]["maskT"]).view(BF)
        mT = np.ascontiguousarray(maskT.reshape(128, 16, SEQ)[:, :, my].reshape(128, 16 * THALF))
        src = oh1[cid]
        k = np.asarray(src["k_s"]).view(BF)
        v_ = np.asarray(src["v_s"]).view(BF)
        if h == 1:
            k = np.ascontiguousarray(k[:, inv1])
            v_ = np.ascontiguousarray(v_[:, inv1])
        im.append({"q_s": np.asarray(src["q_s"]).view(BF), "k_s": k, "v_s": v_,
                   "maskT_my": mT, "wd": wd_p,
                   "rowv": rowv_of(my),
                   "colv": np.arange(SEQ, dtype=np.float32)[None, :],
                   "xr": np.ascontiguousarray(x_cur[b][:, my])})
    x_cur = assemble(_run(nc, im))

    nc_moe_f = build_moe(out_dt=F32)
    im = []
    for cid in range(8):
        b, h, my, _ = _halves(cid)
        im.append({**moe_in, "x": np.ascontiguousarray(x_cur[b][:, my])})
    fin = _run(nc_moe_f, im)
    out = np.empty((B, SEQ, DM), np.float32)
    for cid in range(8):
        b, h, my, _ = _halves(cid)
        out[b, my] = np.asarray(fin[cid]["xout"]).T
    return out
